# version 25
# baseline (speedup 1.0000x reference)
import numpy as np
from contextlib import ExitStack
import concourse.bass as bass
import concourse.mybir as mybir
from concourse.bass_utils import run_bass_kernel_spmd

F32 = mybir.dt.float32
BF16 = mybir.dt.bfloat16
AF = mybir.ActivationFunctionType
ALU = mybir.AluOpType
AX = mybir.AxisListType

B, S, D, H, E, HID = 8, 2048, 1024, 16, 64, 4096
KD = D // 128
KH = HID // 128
NPAIR = H // 2
TC = S // 128
QS = float(8.0 ** -0.5)
LN_EPS = 1e-5
THALF = S // 2

AR = {
    "xbuf":    (0,      65536),
    "xsq":     (65536,  16384),
    "rowbuf1": (81920,  16384),
    "sqbf1":   (98304,  8192),
    "wvld":    (115584, 16384),
    "wkld":    (131968, 16384),
    "wqld":    (148352, 16384),
    "q_sb":    (0,      32768),
    "k_sb":    (32768,  32768),
    "vt_sb":   (65536,  33280),
    "bvrep":   (98816,  4096),
    "selpair": (98816,  4096),
    "q2buf":   (102912, 4096),
    "crow":    (107264, 128),
    "pbuf":    (107392, 8192),
    "dtmp":    (115584, 8192),
    "denom":   (123776, 8192),
    "wo_sb":   (148352, 16384),
    "attn_n":  (0,      32768),
    "x2buf":   (32768,  65536),
    "rtile":   (107392, 8192),
    "xrb":     (131968, 16384),
    "xsq2":    (0,      16384),
    "rowbuf2": (16384,  16384),
    "sqbf2":   (98304,  8192),
    "hbufA":   (0,      65536),
    "hbufB":   (65536,  65536),
    "x2rb":    (131072, 8192),
    "ostg":    (139264, 8192),
    "wb0":     (148352, 8192),
    "wb1":     (156544, 8192),
    "wb2":     (164736, 8192),
}
ARENA_SIZE = 172928


def build_nc():
    nc = bass.Bass("TRN2", target_bir_lowering=False)

    xst = nc.dram_tensor("xst", [128, KD * S], F32, kind="ExternalInput").ap()
    wq_d = nc.dram_tensor("wq", [128, 8192], F32, kind="ExternalInput").ap()
    wk_d = nc.dram_tensor("wk", [128, 8192], F32, kind="ExternalInput").ap()
    wv_d = nc.dram_tensor("wv", [128, 8192], F32, kind="ExternalInput").ap()
    wo_d = nc.dram_tensor("wo", [128, 8192], F32, kind="ExternalInput").ap()
    bq_d = nc.dram_tensor("bq", [128, 8], F32, kind="ExternalInput").ap()
    bk_d = nc.dram_tensor("bk", [128, 8], F32, kind="ExternalInput").ap()
    bv_d = nc.dram_tensor("bvr", [1, 1024], F32, kind="ExternalInput").ap()
    bo_d = nc.dram_tensor("bo", [128, 8], F32, kind="ExternalInput").ap()
    w1_d = nc.dram_tensor("w1", [KH, 128, KD * 128], F32, kind="ExternalInput").ap()
    w2_d = nc.dram_tensor("w2", [KH, 128, KD * 128], F32, kind="ExternalInput").ap()
    h1_d = nc.dram_tensor("h1", [KH, 128, KH * 128], F32, kind="ExternalInput").ap()
    h2_d = nc.dram_tensor("h2", [KH, 128, KH * 128], F32, kind="ExternalInput").ap()
    w3_d = nc.dram_tensor("w3", [KD, 128, KH * 128], F32, kind="ExternalInput").ap()
    s1_d = nc.dram_tensor("s1", [128, KH], F32, kind="ExternalInput").ap()
    s2_d = nc.dram_tensor("s2", [128, KH], F32, kind="ExternalInput").ap()
    sh1_d = nc.dram_tensor("sh1", [128, KH], F32, kind="ExternalInput").ap()
    sh2_d = nc.dram_tensor("sh2", [128, KH], F32, kind="ExternalInput").ap()
    s3_d = nc.dram_tensor("s3", [128, KD], F32, kind="ExternalInput").ap()
    out_fm = nc.dram_tensor("out_fm", [128, KD * S], F32, kind="ExternalOutput").ap()
    x2hbm = nc.dram_tensor("x2hbm", [128, KD * S], F32, kind="Internal").ap()

    ones_c = nc.inline_tensor(np.ones((1, 128), np.float32), "ones_c").ap()
    onescol_c = nc.inline_tensor(np.ones((128, 1), np.float32), "onescol_c").ap()
    sp = np.zeros((16, NPAIR * 128), np.float32)
    for j in range(NPAIR):
        sp[2 * j, j * 128: j * 128 + 64] = 1.0
        sp[2 * j + 1, j * 128 + 64: j * 128 + 128] = 1.0
    selpair_c = nc.inline_tensor(sp, "selpair_c").ap()

    ones_sb = nc.alloc_sbuf_tensor("ones_sb", [1, 128], F32)[:]
    onescol_sb = nc.alloc_sbuf_tensor("onescol_sb", [128, 1], F32)[:]
    onescol_bf = nc.alloc_sbuf_tensor("onescol_bf", [128, 1], BF16)[:]
    bqs = nc.alloc_sbuf_tensor("bqs", [128, 8], F32)[:]
    bks = nc.alloc_sbuf_tensor("bks", [128, 8], F32)[:]
    bo_sb = nc.alloc_sbuf_tensor("bo_sb", [128, 8], F32)[:]
    bvr_sb = nc.alloc_sbuf_tensor("bvr_sb", [1, 1024], F32)[:]
    s1_sb = nc.alloc_sbuf_tensor("s1_sb", [128, KH], F32)[:]
    s2_sb = nc.alloc_sbuf_tensor("s2_sb", [128, KH], F32)[:]
    sh1_sb = nc.alloc_sbuf_tensor("sh1_sb", [128, KH], F32)[:]
    sh2_sb = nc.alloc_sbuf_tensor("sh2_sb", [128, KH], F32)[:]
    s3_sb = nc.alloc_sbuf_tensor("s3_sb", [128, KD], F32)[:]
    cneg = nc.alloc_sbuf_tensor("cneg", [128, NPAIR], F32)[:]
    act_bf = nc.alloc_sbuf_tensor("act_bf", [128, KD * S], BF16)[:]

    arena_base = (nc.sbuf_base + 255) & ~255
    assert arena_base + ARENA_SIZE <= nc.sbuf_top, (
        f"arena overflow: base={arena_base} size={ARENA_SIZE} top={nc.sbuf_top}")

    def arena(name, shape, dtype):
        off, sz = AR[name]
        esz = 2 if dtype == BF16 else 4
        nbytes = int(np.prod(shape[1:])) * esz
        assert nbytes <= sz, (name, nbytes, sz)
        return nc.alloc_sbuf_tensor_at(name, shape, dtype,
                                       offset=arena_base + off)[:]

    xbuf = arena("xbuf", [128, KD * S], F32)
    xsq = arena("xsq", [128, 2 * S], F32)
    sqbf1 = arena("sqbf1", [128, 2 * S], BF16)
    rowbuf1 = arena("rowbuf1", [1, 2 * S], F32)
    wv_sb = arena("wvld", [128, 8192], BF16)
    wk_sb = arena("wkld", [128, 8192], BF16)
    wq_sb = arena("wqld", [128, 8192], BF16)
    q_sb = arena("q_sb", [128, NPAIR * S], BF16)
    k_sb = arena("k_sb", [128, NPAIR * S], BF16)
    vt_sb = arena("vt_sb", [128, TC * 16 * 65], BF16)
    bvrep_sb = arena("bvrep", [128, 1024], F32)
    q2buf = arena("q2buf", [128, S], BF16)
    crow = arena("crow", [1, 32], F32)
    denom_sb = arena("denom", [16, S], F32)
    selpair_sb = arena("selpair", [16, NPAIR * 128], BF16)
    pbuf = arena("pbuf", [128, 2 * S], BF16)
    dtmp = arena("dtmp", [65, 2 * 1024], F32)
    wo_sb = arena("wo_sb", [128, 8192], BF16)
    attn_n = arena("attn_n", [128, NPAIR * S], BF16)
    x2buf = arena("x2buf", [128, KD * S], F32)
    rtile = arena("rtile", [16, S], BF16)
    xrb = arena("xrb", [128, 2 * S], F32)
    xsq2 = arena("xsq2", [128, 2 * S], F32)
    sqbf2 = arena("sqbf2", [128, 2 * S], BF16)
    rowbuf2 = arena("rowbuf2", [1, 2 * S], F32)
    hbufA = arena("hbufA", [128, KH * THALF], BF16)
    hbufB = arena("hbufB", [128, KH * THALF], BF16)
    wb0 = arena("wb0", [128, KH * 128], BF16)
    wb1 = arena("wb1", [128, KH * 128], BF16)
    wb2 = arena("wb2", [128, KH * 128], BF16)
    x2rb = arena("x2rb", [128, 2 * 1024], F32)
    ostg = arena("ostg", [128, 2 * 1024], F32)

    n = {"v": 0, "sc": 0, "pe": 0, "exp": 0, "att": 0, "scx": 0, "dcp": 0,
         "wc": 0, "mm": 0, "evs": 0, "evv": 0}

    es = ExitStack()
    with es:
        s_x = es.enter_context(nc.semaphore("s_x"))
        s_const = es.enter_context(nc.semaphore("s_const"))
        s_wqkv = es.enter_context(nc.semaphore("s_wqkv"))
        s_small = es.enter_context(nc.semaphore("s_small"))
        s_v = es.enter_context(nc.semaphore("s_v"))
        s_sc = es.enter_context(nc.semaphore("s_sc"))
        s_pe = es.enter_context(nc.semaphore("s_pe"))
        s_exp = es.enter_context(nc.semaphore("s_exp"))
        s_att = es.enter_context(nc.semaphore("s_att"))
        s_scx = es.enter_context(nc.semaphore("s_scx"))
        s_dcp = es.enter_context(nc.semaphore("s_dcp"))
        s_wl0 = es.enter_context(nc.semaphore("s_wl0"))
        s_wl1 = es.enter_context(nc.semaphore("s_wl1"))
        s_wl2 = es.enter_context(nc.semaphore("s_wl2"))
        s_wc = es.enter_context(nc.semaphore("s_wc"))
        s_mm = es.enter_context(nc.semaphore("s_mm"))
        s_evs = es.enter_context(nc.semaphore("s_evs"))
        s_evv = es.enter_context(nc.semaphore("s_evv"))
        s_xr0 = es.enter_context(nc.semaphore("s_xr0"))
        s_xr1 = es.enter_context(nc.semaphore("s_xr1"))
        s_o0 = es.enter_context(nc.semaphore("s_o0"))
        s_o1 = es.enter_context(nc.semaphore("s_o1"))
        s_x2st = es.enter_context(nc.semaphore("s_x2st"))
        s_dd0 = es.enter_context(nc.semaphore("s_dd0"))
        s_vt = es.enter_context(nc.semaphore("s_vt"))
        s_qk = es.enter_context(nc.semaphore("s_qk"))
        s_cg = es.enter_context(nc.semaphore("s_cg"))
        s_xe = es.enter_context(nc.semaphore("s_xe"))
        s_dd1 = es.enter_context(nc.semaphore("s_dd1"))

        swl = [s_wl0, s_wl1, s_wl2]
        sxr = [s_xr0, s_xr1]
        sout = [s_o0, s_o1]
        sdd = [s_dd0, s_dd1]

        def vop(v, meth, *a, **kw):
            v.wait_ge(s_v, n["v"])
            getattr(v, meth)(*a, **kw).then_inc(s_v, 1)
            n["v"] += 1
            return n["v"]

        def scop(sc, meth, *a, **kw):
            sc.wait_ge(s_sc, n["sc"])
            getattr(sc, meth)(*a, **kw).then_inc(s_sc, 1)
            n["sc"] += 1
            return n["sc"]

        def emit_layernorm(src, xsqb, sqbf, rowb, tag, normalize=True):
            rowm = rowb[0:1, 0:S]
            rowr = rowb[0:1, S:2 * S]
            rowt = xsqb[0:1, 0:S]
            rowu = xsqb[0:1, S:2 * S]
            with (
                nc.psum_tensor("ps_sx" + tag, [1, S], F32) as ps_sx,
                nc.psum_tensor("ps_sq" + tag, [1, S], F32) as ps_sq,
            ):
                with nc.Block() as blk:
                    pe0ln = n["pe"]

                    @blk.scalar
                    def _(sc):
                        for k in range(KD):
                            if k >= 2:
                                sc.wait_ge(s_pe, pe0ln + k - 1)
                            scop(sc, "activation", sqbf[:, (k % 2) * S:(k % 2) * S + S],
                                 src[:, k * S:(k + 1) * S], AF.Square)

                    sc0 = n["sc"] - KD

                    @blk.tensor
                    def _(t):
                        t.wait_ge(s_const, 32)
                        t.wait_ge(s_small, 16 * 10)
                        for k in range(KD):
                            for nn in range(4):
                                t.matmul(ps_sx[:, nn * 512:(nn + 1) * 512], onescol_sb,
                                         src[:, k * S + nn * 512: k * S + (nn + 1) * 512],
                                         start=(k == 0), stop=(k == KD - 1))
                            t.wait_ge(s_sc, sc0 + k + 1)
                            ins = None
                            for nn in range(4):
                                ins = t.matmul(
                                    ps_sq[:, nn * 512:(nn + 1) * 512], onescol_bf,
                                    sqbf[:, (k % 2) * S + nn * 512: (k % 2) * S + (nn + 1) * 512],
                                    start=(k == 0), stop=(k == KD - 1))
                            ins.then_inc(s_pe, 1)

                    n["pe"] += KD

                    @blk.vector
                    def _(v):
                        v.wait_ge(s_pe, n["pe"])
                        vop(v, "tensor_scalar_mul", rowm, ps_sx[:], 1.0 / D)
                        vop(v, "tensor_mul", rowu, rowm, rowm)
                        vop(v, "scalar_tensor_tensor", rowr, ps_sq[:], 1.0 / D,
                            rowu, ALU.mult, ALU.subtract)
                        vop(v, "tensor_scalar_add", rowu, rowr, LN_EPS)

                    @blk.scalar
                    def _(sc):
                        sc.wait_ge(s_v, n["v"])
                        scop(sc, "activation", rowt, rowu, AF.Ln)
                        scop(sc, "activation", rowt, rowt, AF.Exp, scale=-0.5)

                    @blk.vector
                    def _(v):
                        v.wait_ge(s_sc, n["sc"])
                        vop(v, "tensor_mul", rowr, rowt, rowt)
                        vop(v, "tensor_mul", rowr, rowr, rowu)
                        vop(v, "tensor_scalar", rowr, rowr, -0.5, 1.5,
                            ALU.mult, ALU.add)
                        vop(v, "tensor_mul", rowr, rowt, rowr)

            if not normalize:
                return n["v"]
            with (
                nc.psum_tensor("ps_mrep" + tag, [128, S], F32) as ps_mrep,
                nc.psum_tensor("ps_rrep" + tag, [128, S], F32) as ps_rrep,
            ):
                with nc.Block() as blk:
                    @blk.tensor
                    def _(t):
                        ins = None
                        for nn in range(4):
                            t.matmul(ps_mrep[:, nn * 512:(nn + 1) * 512], ones_sb,
                                     rowm[0:1, nn * 512:(nn + 1) * 512],
                                     start=True, stop=True)
                        for nn in range(4):
                            ins = t.matmul(ps_rrep[:, nn * 512:(nn + 1) * 512], ones_sb,
                                           rowr[0:1, nn * 512:(nn + 1) * 512],
                                           start=True, stop=True)
                        ins.then_inc(s_pe, 1)

                    n["pe"] += 1

                    @blk.vector
                    def _(v):
                        v.wait_ge(s_pe, n["pe"])
                        for k in range(KD):
                            vop(v, "tensor_sub", xsqb[:, (k % 2) * S:(k % 2) * S + S],
                                src[:, k * S:(k + 1) * S], ps_mrep[:])
                            vop(v, "tensor_mul", act_bf[:, k * S:(k + 1) * S],
                                xsqb[:, (k % 2) * S:(k % 2) * S + S], ps_rrep[:])

        with nc.Block() as blk:
            @blk.gpsimd
            def _(g):
                g.dma_start(onescol_bf, onescol_c).then_inc(s_small, 16)
                g.dma_start(bqs, bq_d).then_inc(s_small, 16)
                g.dma_start(bks, bk_d).then_inc(s_small, 16)
                g.dma_start(bo_sb, bo_d).then_inc(s_small, 16)
                g.dma_start(bvr_sb, bv_d).then_inc(s_small, 16)
                g.dma_start(s1_sb, s1_d).then_inc(s_small, 16)
                g.dma_start(s2_sb, s2_d).then_inc(s_small, 16)
                g.dma_start(sh1_sb, sh1_d).then_inc(s_small, 16)
                g.dma_start(sh2_sb, sh2_d).then_inc(s_small, 16)
                g.dma_start(s3_sb, s3_d).then_inc(s_small, 16)
                g.wait_ge(s_x, 16)
                g.dma_start(wq_sb, wq_d).then_inc(s_wqkv, 16)
                g.dma_start(wk_sb, wk_d).then_inc(s_wqkv, 16)
                g.dma_start(wv_sb, wv_d).then_inc(s_wqkv, 16)

            @blk.sync
            def _(sy):
                sy.dma_start(xbuf, xst).then_inc(s_x, 16)
                sy.dma_start(ones_sb, ones_c).then_inc(s_const, 16)
                sy.dma_start(onescol_sb, onescol_c).then_inc(s_const, 16)

            @blk.vector
            def _(v):
                v.wait_ge(s_x, 16)
                v.wait_ge(s_small, 16 * 10)
                vop(v, "tensor_scalar_mul", bqs, bqs, QS)
                vop(v, "tensor_scalar_mul", bks, bks, QS)

            @blk.tensor
            def _(t):
                t.wait_ge(s_x, 16)

        vrows1 = emit_layernorm(xbuf, xsq, sqbf1, rowbuf1, "a", normalize=False)

        TQ = S // 4
        vt3 = vt_sb.rearrange("p (s e) -> p s e", e=65)
        ILIST = [(j, qq) for j in range(NPAIR) for qq in range(4)]
        NI = len(ILIST)
        exp0 = n["exp"]
        att0 = n["att"]
        scx0 = n["scx"]
        dcp0 = n["dcp"]
        fence_vt = crow[0:1, 8:9]
        fence_qk = crow[0:1, 11:12]
        fence_cg = crow[0:1, 12:13]

        with nc.psum_tensor("ps_bv", [128, 1024], F32) as ps_bv:
            with nc.Block() as blk:
                @blk.tensor
                def _(t):
                    t.wait_ge(s_small, 16 * 10)
                    ins = None
                    for nn in range(2):
                        ins = t.matmul(ps_bv[:, nn * 512:(nn + 1) * 512], ones_sb,
                                       bvr_sb[0:1, nn * 512:(nn + 1) * 512],
                                       start=True, stop=True)
                    ins.then_inc(s_pe, 1)

                n["pe"] += 1

                @blk.vector
                def _(v):
                    v.wait_ge(s_pe, n["pe"])
                    vop(v, "tensor_copy", bvrep_sb, ps_bv[:])

        with (
            nc.psum_tensor("ps_sc", [128, 2 * 1024], F32) as ps_sc_g,
            nc.psum_tensor("ps_aE", [65, TQ], F32) as ps_aE_g,
            nc.psum_tensor("ps_aO", [65, TQ], F32) as ps_aO_g,
            nc.psum_tensor("ps_x", [128, 1024], F32) as ps_x_g,
        ):
            ps_sc = ps_sc_g[:]
            ps_aE = ps_aE_g[:]
            ps_aO = ps_aO_g[:]
            ps_x = ps_x_g[:]

            xu = {"n": 0, "xe": 0}
            dtmp_vals = {}
            proj_pe = {}
            vch_pe = {}
            s2q_val = [None] * (NPAIR + 1)
            s2k_val = [None] * (NPAIR + 1)
            pe_knrm = [0] * (NPAIR + 1)
            pe_crep = [0] * (NPAIR + 1)

            def pe_use_slot(t):
                if xu["n"] > 0:
                    t.wait_ge(s_xe, xu["n"])

            def emit_proj(t, j, which, hh):
                wsb = wq_sb if which == "q" else wk_sb
                pe_use_slot(t)
                ins = None
                for k in range(KD):
                    for nn in range(2):
                        ins = t.matmul(
                            ps_x[:, nn * 512:(nn + 1) * 512],
                            wsb[:, j * 1024 + k * 128: j * 1024 + (k + 1) * 128],
                            act_bf[:, k * S + hh * 1024 + nn * 512: k * S + hh * 1024 + (nn + 1) * 512],
                            start=(k == 0), stop=(k == KD - 1))
                ins.then_inc(s_pe, 1)
                n["pe"] += 1
                xu["n"] += 1
                proj_pe[(j, which, hh)] = n["pe"]

            def emit_vchunk(t, c):
                pe_use_slot(t)
                ins = None
                for k in range(KD):
                    for nn in range(2):
                        ins = t.matmul(
                            ps_x[:, nn * 512:(nn + 1) * 512],
                            act_bf[:, k * S + c * 128: k * S + (c + 1) * 128],
                            wv_sb[:, k * 1024 + nn * 512: k * 1024 + (nn + 1) * 512],
                            start=(k == 0), stop=(k == KD - 1))
                ins.then_inc(s_pe, 1)
                n["pe"] += 1
                xu["n"] += 1
                vch_pe[c] = n["pe"]

            def emit_scores(t, I, c):
                j, qq = ILIST[I]
                if c == 0 and qq == 0:
                    t.wait_ge(s_qk, j + 1)
                wv_ = exp0 + I * TC + c - 1
                if wv_ > 0:
                    t.wait_ge(s_exp, wv_)
                bufo = (c % 2) * 1024
                ins = None
                for hh in range(2):
                    p0 = hh * 64
                    ins = t.matmul(
                        ps_sc[:, bufo + hh * TQ: bufo + (hh + 1) * TQ],
                        k_sb[p0:p0 + 64, j * S + c * 128: j * S + (c + 1) * 128],
                        q_sb[p0:p0 + 64, j * S + qq * TQ: j * S + (qq + 1) * TQ],
                        start=True, stop=True)
                ins.then_inc(s_scx, 1)

            def emit_att(t, I, c, exp_wait=False):
                j, qq = ILIST[I]
                if exp_wait:
                    t.wait_ge(s_exp, exp0 + I * TC + c + 1)
                if c == 0 and I >= 1:
                    t.wait_ge(s_dcp, dcp0 + I)
                if I == 0:
                    t.wait_ge(s_vt, c + 1)
                elif I == 1 and c == 0:
                    t.wait_ge(s_vt, TC)
                sb = (c % 4) * 1024
                ins = None
                for hh, ps_a in ((0, ps_aE), (1, ps_aO)):
                    vslot = (c * 16 + 2 * j + hh) * 65
                    ins = t.matmul(
                        ps_a[:, :],
                        vt_sb[:, vslot: vslot + 65],
                        pbuf[:, sb + hh * TQ: sb + (hh + 1) * TQ],
                        start=(c == 0), stop=(c == TC - 1))
                ins.then_inc(s_att, 1)

            n["scx"] += NI * TC
            n["att"] += NI * TC
            n["exp"] += NI * TC

            with nc.Block() as blk:
                def emit_evac_one(v, j, w, hh):
                    v.wait_ge(s_pe, proj_pe[(j, w, hh)])
                    dst = q_sb if w == "q" else k_sb
                    bias = bqs if w == "q" else bks
                    v.wait_ge(s_v, n["v"])
                    v.tensor_scalar(
                        dst[:, j * S + hh * 1024: j * S + (hh + 1) * 1024],
                        ps_x[:], QS, bias[:, j:j + 1],
                        ALU.mult, ALU.add).then_inc(s_xe, 1)
                    xu["xe"] += 1

                def emit_qk_tail(v, j):
                    v.wait_ge(s_xe, xu["xe"])
                    if j >= 1:
                        v.wait_ge(s_qk, j)
                    v.wait_ge(s_v, n["v"])
                    v.tensor_copy(fence_qk, k_sb[0:1, j * S: j * S + 1]).then_inc(s_qk, 1)
                    if j >= 1:
                        v.wait_ge(s_pe, pe_knrm[j - 1])
                    s2q_val[j] = vop(v, "tensor_mul", q2buf,
                                     q_sb[:, j * S:(j + 1) * S],
                                     q_sb[:, j * S:(j + 1) * S])

                def emit_norm(t, j, th, sval):
                    t.wait_ge(s_v, sval)
                    pe_use_slot(t)
                    ins = None
                    for nn in range(2):
                        ins = t.matmul(
                            ps_x[0:1, nn * 512:(nn + 1) * 512], onescol_bf,
                            q2buf[:, th * 1024 + nn * 512: th * 1024 + (nn + 1) * 512],
                            start=True, stop=True)
                    ins.then_inc(s_pe, 1)
                    n["pe"] += 1
                    xu["n"] += 1
                    return n["pe"]

                def v_reduce(v, pe_val, col):
                    v.wait_ge(s_pe, pe_val)
                    v.wait_ge(s_xe, xu["xe"])
                    v.wait_ge(s_v, n["v"])
                    v.reduce_max(crow[0:1, col:col + 1], ps_x[0:1, 0:1024],
                                 axis=AX.X).then_inc(s_xe, 1)
                    xu["xe"] += 1

                def emit_crep_T(t, j):
                    t.wait_ge(s_v, cscale_val[j])
                    pe_use_slot(t)
                    t.matmul(ps_x[:, 0:1], ones_sb, crow[0:1, 4:5],
                             start=True, stop=True).then_inc(s_pe, 1)
                    n["pe"] += 1
                    xu["n"] += 1
                    pe_crep[j] = n["pe"]

                def emit_cneg_V(v, j):
                    v.wait_ge(s_pe, pe_crep[j])
                    v.wait_ge(s_xe, xu["xe"])
                    v.wait_ge(s_v, n["v"])
                    v.tensor_copy(cneg[:, j:j + 1],
                                  ps_x[:, 0:1]).then_inc(s_xe, 1)
                    xu["xe"] += 1
                    v.wait_ge(s_xe, xu["xe"])
                    if j >= 1:
                        v.wait_ge(s_cg, j)
                    v.wait_ge(s_v, n["v"])
                    v.tensor_copy(fence_cg, cneg[0:1, j:j + 1]).then_inc(s_cg, 1)

                qn_pe = {}
                kn_pe = {}
                cscale_val = [None] * (NPAIR + 1)

                def c_qn_V(v, j):
                    v_reduce(v, qn_pe[(j, 0)], 9)
                    v_reduce(v, qn_pe[(j, 1)], 10)
                    v.wait_ge(s_xe, xu["xe"])
                    vop(v, "tensor_max", crow[0:1, 5:6],
                        crow[0:1, 9:10], crow[0:1, 10:11])
                    v.wait_ge(s_pe, qn_pe[(j, 1)])
                    s2k_val[j] = vop(v, "tensor_mul", q2buf,
                                     k_sb[:, j * S:(j + 1) * S],
                                     k_sb[:, j * S:(j + 1) * S])

                def c_kn_V(v, j):
                    v_reduce(v, kn_pe[(j, 0)], 9)
                    v_reduce(v, kn_pe[(j, 1)], 10)
                    v.wait_ge(s_xe, xu["xe"])
                    vop(v, "tensor_max", crow[0:1, 6:7],
                        crow[0:1, 9:10], crow[0:1, 10:11])
                    if j >= 1:
                        v.wait_ge(s_pe, pe_crep[j - 1])
                    vop(v, "tensor_add", crow[0:1, 4:5],
                        crow[0:1, 5:6], crow[0:1, 6:7])
                    cscale_val[j] = vop(v, "tensor_scalar_mul", crow[0:1, 4:5],
                                        crow[0:1, 4:5], -0.505)

                rowm1 = rowbuf1[0:1, 0:S]
                rowr1 = rowbuf1[0:1, S:2 * S]
                vn1h = [0, 0]
                relcell = crow[0:1, 13:14]

                def emit_rep1(t, row, hf):
                    pe_use_slot(t)
                    ins = None
                    for nn in range(2):
                        ins = t.matmul(
                            ps_x[:, nn * 512:(nn + 1) * 512], ones_sb,
                            row[0:1, hf * 1024 + nn * 512: hf * 1024 + (nn + 1) * 512],
                            start=True, stop=True)
                    ins.then_inc(s_pe, 1)
                    n["pe"] += 1
                    xu["n"] += 1
                    return n["pe"]

                def slot_release(v):
                    v.wait_ge(s_xe, xu["xe"])
                    v.wait_ge(s_v, n["v"])
                    v.tensor_copy(relcell, bvr_sb[0:1, 0:1]).then_inc(s_xe, 1)
                    xu["xe"] += 1

                def emit_vchunk2(t, c):
                    if c >= 2:
                        t.wait_ge(s_vt, c - 1)
                    ins = None
                    for k in range(KD):
                        for nn in range(2):
                            ins = t.matmul(
                                ps_sc[:, (c % 2) * 1024 + nn * 512:
                                      (c % 2) * 1024 + (nn + 1) * 512],
                                act_bf[:, k * S + c * 128: k * S + (c + 1) * 128],
                                wv_sb[:, k * 1024 + nn * 512: k * 1024 + (nn + 1) * 512],
                                start=(k == 0), stop=(k == KD - 1))
                    ins.then_inc(s_pe, 1)
                    n["pe"] += 1
                    vch_pe[c] = n["pe"]

                def emit_vt_evac(v, c):
                    v.wait_ge(s_pe, vch_pe[c])
                    v.wait_ge(s_v, n["v"])
                    v.tensor_add(
                        vt3[:, c * 16:(c + 1) * 16, 0:64],
                        ps_sc[:, (c % 2) * 1024:(c % 2) * 1024 + 1024]
                        .rearrange("p (h e) -> p h e", e=64),
                        bvrep_sb.rearrange("p (h e) -> p h e", e=64)
                    ).then_inc(s_v, 1)
                    n["v"] += 1
                    v.wait_ge(s_v, n["v"])
                    if c >= 1:
                        v.wait_ge(s_vt, c)
                    v.tensor_copy(
                        fence_vt, vt_sb[0:1, c * 16 * 65: c * 16 * 65 + 1]
                    ).then_inc(s_vt, 1)

                @blk.tensor
                def _(t):
                    t.wait_ge(s_v, vrows1)
                    emit_rep1(t, rowm1, 0)

                @blk.vector
                def _(v, pv=n["pe"]):
                    vop(v, "memset", vt3[:, 0:112, 64:65], 1.0)
                    v.wait_ge(s_pe, pv)
                    for k in range(KD):
                        vop(v, "tensor_sub", act_bf[:, k * S: k * S + 1024],
                            xbuf[:, k * S: k * S + 1024], ps_x[:])
                    slot_release(v)

                @blk.tensor
                def _(t):
                    emit_rep1(t, rowr1, 0)

                @blk.vector
                def _(v, pv=n["pe"]):
                    v.wait_ge(s_pe, pv)
                    for k in range(KD):
                        vop(v, "tensor_mul", act_bf[:, k * S: k * S + 1024],
                            act_bf[:, k * S: k * S + 1024], ps_x[:])
                    vn1h[0] = n["v"]
                    slot_release(v)

                @blk.tensor
                def _(t):
                    emit_rep1(t, rowm1, 1)

                @blk.vector
                def _(v, pv=n["pe"]):
                    v.wait_ge(s_pe, pv)
                    for k in range(KD):
                        vop(v, "tensor_sub",
                            act_bf[:, k * S + 1024: k * S + 2048],
                            xbuf[:, k * S + 1024: k * S + 2048], ps_x[:])
                    slot_release(v)

                @blk.tensor
                def _(t):
                    t.wait_ge(s_wqkv, 16 * 3)
                    t.wait_ge(s_v, vn1h[0])
                    emit_proj(t, 0, "q", 0)
                    emit_proj(t, 0, "k", 0)
                    for c in range(KD - 1):
                        emit_vchunk2(t, c)

                @blk.vector
                def _(v):
                    emit_evac_one(v, 0, "q", 0)
                    emit_evac_one(v, 0, "k", 0)
                    for c in range(KD - 1):
                        emit_vt_evac(v, c)

                @blk.tensor
                def _(t):
                    emit_vchunk2(t, KD - 1)
                    pe_r1 = emit_rep1(t, rowr1, 1)

                @blk.vector
                def _(v):
                    emit_vt_evac(v, KD - 1)
                    v.wait_ge(s_pe, n["pe"])
                    v.wait_ge(s_v, n["v"])
                    v.memset(vt3[:, 112:256, 64:65], 1.0).then_inc(s_v, 1)
                    n["v"] += 1
                    for k in range(KD):
                        vop(v, "tensor_mul",
                            act_bf[:, k * S + 1024: k * S + 2048],
                            act_bf[:, k * S + 1024: k * S + 2048], ps_x[:])
                    vn1h[1] = n["v"]
                    slot_release(v)

                @blk.tensor
                def _(t):
                    t.wait_ge(s_v, vn1h[1])
                    emit_proj(t, 0, "q", 1)
                    emit_proj(t, 0, "k", 1)
                    for c in range(KD, TC):
                        emit_vchunk2(t, c)

                @blk.vector
                def _(v):
                    emit_evac_one(v, 0, "q", 1)
                    emit_evac_one(v, 0, "k", 1)
                    for c in range(KD, TC):
                        emit_vt_evac(v, c)
                    emit_qk_tail(v, 0)

                @blk.tensor
                def _(t):
                    qn_pe[(0, 0)] = emit_norm(t, 0, 0, s2q_val[0])
                    qn_pe[(0, 1)] = emit_norm(t, 0, 1, s2q_val[0])

                @blk.vector
                def _(v):
                    c_qn_V(v, 0)

                @blk.tensor
                def _(t):
                    kn_pe[(0, 0)] = emit_norm(t, 0, 0, s2k_val[0])
                    kn_pe[(0, 1)] = emit_norm(t, 0, 1, s2k_val[0])
                    pe_knrm[0] = n["pe"]

                @blk.vector
                def _(v):
                    c_kn_V(v, 0)

                @blk.tensor
                def _(t):
                    emit_crep_T(t, 0)

                @blk.vector
                def _(v):
                    emit_cneg_V(v, 0)

                for j in range(NPAIR):
                    I0 = 4 * j
                    jn = j + 1
                    have_next = jn < NPAIR

                    if j == NPAIR - 1:
                        @blk.gpsimd
                        def _(g):
                            g.wait_ge(s_pe, proj_pe[(NPAIR - 1, "k", 1)])
                            g.dma_start(wo_sb, wo_d).then_inc(s_wqkv, 16)

                    for qq in range(4):
                        I = I0 + qq

                        @blk.tensor
                        def _(t, j=j, qq=qq, I=I, jn=jn, have_next=have_next):
                            side = {}
                            if qq == 0 and have_next:
                                for gi, (w, hh) in enumerate(
                                        (("q", 0), ("q", 1), ("k", 0), ("k", 1))):
                                    side.setdefault(gi * 4, []).append(
                                        ("proj", jn, w, hh))
                            elif qq == 1 and have_next:
                                side[1] = [("qn", jn, 0)]
                                side[3] = [("qn", jn, 1)]
                            elif qq == 2 and have_next:
                                side[1] = [("kn", jn, 0)]
                                side[3] = [("kn", jn, 1)]
                            elif qq == 3 and have_next:
                                side[1] = [("cr", jn)]
                            for c in range(TC):
                                emit_scores(t, I, c)
                                if c <= 1 and I >= 1:
                                    emit_att(t, I - 1, TC - 2 + c)
                                if c >= 2:
                                    emit_att(t, I, c - 2)
                                for item in side.get(c, ()):
                                    kind = item[0]
                                    if kind == "v":
                                        emit_vchunk(t, item[1])
                                    elif kind == "proj":
                                        emit_proj(t, item[1], item[2], item[3])
                                    elif kind == "qn":
                                        qn_pe[(item[1], item[2])] = emit_norm(
                                            t, item[1], item[2], s2q_val[item[1]])
                                    elif kind == "kn":
                                        kn_pe[(item[1], item[2])] = emit_norm(
                                            t, item[1], item[2], s2k_val[item[1]])
                                        if item[2] == 1:
                                            pe_knrm[item[1]] = n["pe"]
                                    else:
                                        emit_crep_T(t, item[1])
                            if I == NI - 1:
                                emit_att(t, I, TC - 2, exp_wait=True)
                                emit_att(t, I, TC - 1, exp_wait=True)

                        @blk.vector
                        def _(v, j=j, qq=qq, I=I, jn=jn, have_next=have_next):
                            if qq == 0 and have_next:
                                for w, hh in (("q", 0), ("q", 1),
                                              ("k", 0), ("k", 1)):
                                    emit_evac_one(v, jn, w, hh)
                                emit_qk_tail(v, jn)
                            elif qq == 1 and have_next:
                                c_qn_V(v, jn)
                            elif qq == 2 and have_next:
                                c_kn_V(v, jn)
                            elif qq == 3 and have_next:
                                emit_cneg_V(v, jn)
                            jj, qqq = ILIST[I]
                            v.wait_ge(s_att, att0 + (I + 1) * TC)
                            for kk, ps_a in ((2 * I, ps_aE), (2 * I + 1, ps_aO)):
                                if kk >= 2:
                                    v.wait_ge(sdd[kk % 2], 16 * (kk // 2))
                                dtmp_vals[kk] = vop(
                                    v, "tensor_copy",
                                    dtmp[64:65, (kk % 2) * 1024:(kk % 2) * 1024 + TQ],
                                    ps_a[64:65, :])
                            vop(v, "tensor_copy",
                                q_sb[0:64, jj * S + qqq * TQ: jj * S + (qqq + 1) * TQ],
                                ps_aE[0:64, :])
                            v.wait_ge(s_v, n["v"])
                            v.tensor_copy(
                                q_sb[64:128, jj * S + qqq * TQ: jj * S + (qqq + 1) * TQ],
                                ps_aO[0:64, :]).then_inc(s_dcp, 1)
                            n["dcp"] += 1

                    @blk.scalar
                    def _(sc, j=j, I0=I0):
                        for qq in range(4):
                            I = I0 + qq
                            for c in range(TC):
                                if qq == 0 and c == 0:
                                    sc.wait_ge(s_cg, j + 1)
                                sc.wait_ge(s_scx, scx0 + I * TC + c + 1)
                                wv_ = att0 + I * TC + c - 3
                                if wv_ > 0:
                                    sc.wait_ge(s_att, wv_)
                                sc.activation(
                                    pbuf[:, (c % 4) * 1024:(c % 4) * 1024 + 1024],
                                    ps_sc[:, (c % 2) * 1024:(c % 2) * 1024 + 1024],
                                    AF.Exp,
                                    bias=cneg[:, j:j + 1]).then_inc(s_exp, 1)

                    if j == 0:
                        @blk.gpsimd
                        def _(g):
                            g.wait_ge(s_vt, TC)
                            g.dma_start(selpair_sb, selpair_c).then_inc(s_small, 16)

                    @blk.sync
                    def _(sy, j=j, I0=I0):
                        for qq in range(4):
                            I = I0 + qq
                            jj, qqq = ILIST[I]
                            for kk, row in ((2 * I, 2 * jj), (2 * I + 1, 2 * jj + 1)):
                                sy.wait_ge(s_v, dtmp_vals[kk])
                                if kk >= 2:
                                    sy.wait_ge(sdd[kk % 2], 16 * (kk // 2))
                                sy.dma_start(
                                    denom_sb[row:row + 1, qqq * TQ:(qqq + 1) * TQ],
                                    dtmp[64:65, (kk % 2) * 1024:(kk % 2) * 1024 + TQ]
                                ).then_inc(sdd[kk % 2], 16)

        with nc.psum_tensor("ps_rep", [128, 2 * S], F32) as ps_rep:
            with nc.Block() as blk:
                rhv = [0, 0]

                @blk.vector
                def _(v):
                    v.wait_ge(s_dd0, 16 * 32)
                    v.wait_ge(s_dd1, 16 * 32)
                    for hf in range(2):
                        vop(v, "reciprocal", denom_sb[:, hf * 1024:(hf + 1) * 1024],
                            denom_sb[:, hf * 1024:(hf + 1) * 1024])
                        rhv[hf] = vop(v, "tensor_copy",
                                      rtile[:, hf * 1024:(hf + 1) * 1024],
                                      denom_sb[:, hf * 1024:(hf + 1) * 1024])

                v0 = n["v"]
                pe0 = n["pe"]

                @blk.tensor
                def _(t):
                    t.wait_ge(s_small, 16 * 11)
                    for j in range(NPAIR):
                        if j >= 2:
                            t.wait_ge(s_v, v0 + j - 1)
                        ins = None
                        for nn in range(4):
                            t.wait_ge(s_v, rhv[nn // 2])
                            ins = t.matmul(
                                ps_rep[:, (j % 2) * S + nn * 512:(j % 2) * S + (nn + 1) * 512],
                                selpair_sb[:, j * 128:(j + 1) * 128],
                                rtile[:, nn * 512:(nn + 1) * 512],
                                start=True, stop=True)
                        ins.then_inc(s_pe, 1)

                n["pe"] += NPAIR

                @blk.vector
                def _(v):
                    for j in range(NPAIR):
                        v.wait_ge(s_pe, pe0 + j + 1)
                        vop(v, "tensor_mul", attn_n[:, j * S:(j + 1) * S],
                            q_sb[:, j * S:(j + 1) * S],
                            ps_rep[:, (j % 2) * S:(j % 2) * S + S])

        with nc.psum_tensor("ps_stats", [33, S], F32) as ps_stats:
            with nc.psum_tensor("ps_wo", [128, 2 * 1024], F32) as ps_wo:
                with nc.Block() as blk:
                    v0 = n["v"]
                    pe0 = n["pe"]
                    sq0 = n["sc"]
                    sqstat_pe = {}

                    @blk.gpsimd
                    def _(g):
                        g.wait_ge(s_pe, pe0 + 26)
                        g.dma_start(wb0[:, 0: KD * 128], w1_d[0]).then_inc(s_wl0, 16)
                        g.dma_start(wb1[:, 0: KD * 128], w1_d[1]).then_inc(s_wl1, 16)
                        g.dma_start(wb2[:, 0: KD * 128], w1_d[2]).then_inc(s_wl2, 16)

                    @blk.sync
                    def _(sy):
                        for m in range(KD):
                            if m >= 2:
                                sy.wait_ge(s_v, v0 + 2 * (m - 1))
                            sy.dma_start(xrb[:, (m % 2) * S:(m % 2) * S + S],
                                         xst[:, m * S:(m + 1) * S]).then_inc(sxr[m % 2], 16)

                    @blk.tensor
                    def _(t):
                        t.wait_ge(s_wqkv, 16 * 4)

                        def emit_stats_sx(k):
                            t.wait_ge(s_v, v0 + 2 * k + 2)
                            ins = None
                            for nn in range(4):
                                ins = t.matmul(ps_stats[0:1, nn * 512:(nn + 1) * 512],
                                               onescol_sb,
                                               x2buf[:, k * S + nn * 512: k * S + (nn + 1) * 512],
                                               start=(k == 0), stop=(k == KD - 1))
                            ins.then_inc(s_pe, 1)
                            n["pe"] += 1

                        def emit_stats_sq(k):
                            t.wait_ge(s_sc, sq0 + k + 1)
                            ins = None
                            for nn in range(4):
                                ins = t.matmul(ps_stats[32:33, nn * 512:(nn + 1) * 512],
                                               onescol_bf,
                                               sqbf2[:, (k % 2) * S + nn * 512: (k % 2) * S + (nn + 1) * 512],
                                               start=(k == 0), stop=(k == KD - 1))
                            ins.then_inc(s_pe, 1)
                            n["pe"] += 1
                            sqstat_pe[k] = n["pe"]

                        for m in range(KD):
                            for hf in range(2):
                                g2 = 2 * m + hf
                                if g2 >= 2:
                                    t.wait_ge(s_v, v0 + g2 - 1)
                                ins = None
                                for j in range(NPAIR):
                                    for nn in range(2):
                                        ins = t.matmul(
                                            ps_wo[:, (g2 % 2) * 1024 + nn * 512:(g2 % 2) * 1024 + (nn + 1) * 512],
                                            wo_sb[:, m * 1024 + j * 128: m * 1024 + (j + 1) * 128],
                                            attn_n[:, j * S + hf * 1024 + nn * 512: j * S + hf * 1024 + (nn + 1) * 512],
                                            start=(j == 0), stop=(j == NPAIR - 1))
                                ins.then_inc(s_pe, 1)
                                n["pe"] += 1
                            if m >= 1:
                                emit_stats_sx(m - 1)
                            if m >= 3:
                                emit_stats_sq(m - 3)
                        emit_stats_sx(KD - 1)
                        for k in range(KD - 3, KD):
                            emit_stats_sq(k)

                    @blk.vector
                    def _(v):
                        for m in range(KD):
                            for hf in range(2):
                                g2 = 2 * m + hf
                                v.wait_ge(s_pe, pe0 + g2 + 1 + max(m - 1, 0) + max(m - 3, 0))
                                v.wait_ge(sxr[m % 2], 16 * (m // 2 + 1))
                                vop(v, "scalar_tensor_tensor",
                                    x2buf[:, m * S + hf * 1024: m * S + hf * 1024 + 1024],
                                    ps_wo[:, (g2 % 2) * 1024:(g2 % 2) * 1024 + 1024],
                                    bo_sb[:, m:m + 1],
                                    xrb[:, (m % 2) * S + hf * 1024:(m % 2) * S + (hf + 1) * 1024],
                                    ALU.add, ALU.add)

                    @blk.scalar
                    def _(sc):
                        for k in range(KD):
                            sc.wait_ge(s_v, v0 + 2 * k + 2)
                            if k >= 2:
                                sc.wait_ge(s_pe, sqstat_pe[k - 2])
                            scop(sc, "activation", sqbf2[:, (k % 2) * S:(k % 2) * S + S],
                                 x2buf[:, k * S:(k + 1) * S], AF.Square)

                    @blk.sync
                    def _(sy):
                        for m in range(KD):
                            sy.wait_ge(s_v, v0 + 2 * m + 2)
                            sy.dma_start(x2hbm[:, m * S:(m + 1) * S],
                                         x2buf[:, m * S:(m + 1) * S]).then_inc(s_x2st, 16)

            with nc.Block() as blk:
                    @blk.vector
                    def _(v):
                        v.wait_ge(s_pe, n["pe"])
                        rowm = rowbuf2[0:1, 0:S]
                        rowr = rowbuf2[0:1, S:2 * S]
                        rowu = xsq2[0:1, S:2 * S]
                        vop(v, "tensor_scalar_mul", rowm, ps_stats[0:1, :], 1.0 / D)
                        vop(v, "tensor_mul", rowu, rowm, rowm)
                        vop(v, "scalar_tensor_tensor", rowr, ps_stats[32:33, :], 1.0 / D,
                            rowu, ALU.mult, ALU.subtract)
                        vop(v, "tensor_scalar_add", rowu, rowr, LN_EPS)

                    @blk.scalar
                    def _(sc):
                        rowt = xsq2[0:1, 0:S]
                        rowu = xsq2[0:1, S:2 * S]
                        sc.wait_ge(s_v, n["v"])
                        scop(sc, "activation", rowt, rowu, AF.Ln)
                        scop(sc, "activation", rowt, rowt, AF.Exp, scale=-0.5)

                    @blk.vector
                    def _(v):
                        rowt = xsq2[0:1, 0:S]
                        rowu = xsq2[0:1, S:2 * S]
                        rowr = rowbuf2[0:1, S:2 * S]
                        v.wait_ge(s_sc, n["sc"])
                        vop(v, "tensor_mul", rowr, rowt, rowt)
                        vop(v, "tensor_mul", rowr, rowr, rowu)
                        vop(v, "tensor_scalar", rowr, rowr, -0.5, 1.5, ALU.mult, ALU.add)
                        vop(v, "tensor_mul", rowr, rowt, rowr)

                    vrows = n["v"]

        wb = [wb0, wb1, wb2]
        sched = []
        for hf in range(2):
            sched += [("w1", hf, m) for m in range(KH)]
            sched += [("w2", hf, m) for m in range(KH)]
            sched += [("h1", hf, m) for m in range(KH)]
            sched += [("h2", hf, m) for m in range(KH)]
            sched += [("w3", hf, m) for m in range(KD)]
        NS = len(sched)
        wsrc = {"w1": (w1_d, KD), "w2": (w2_d, KD), "h1": (h1_d, KH),
                "h2": (h2_d, KH), "w3": (w3_d, KH)}
        eng = {"w1": "s", "w2": "v", "h1": "s", "h2": "s", "w3": "v"}
        evs_b = [0] * (NS + 1)
        evv_b = [0] * (NS + 1)
        for i, (lay, hf, m) in enumerate(sched):
            evs_b[i + 1] = evs_b[i] + (1 if eng[lay] == "s" else 0)
            evv_b[i + 1] = evv_b[i] + (1 if eng[lay] == "v" else 0)

        mm0 = n["mm"]
        wc0 = n["wc"]
        evs0 = n["evs"]
        evv0 = n["evv"]
        w3_evv_val = [evv0 + evv_b[i] + 1
                      for i, (lay, hf, m) in enumerate(sched) if lay == "w3"]

        with (
            nc.psum_tensor("ps_f", [128, 2 * 1024], F32) as ps_f,
            nc.psum_tensor("ps_m2", [128, 1024], F32) as ps_m2,
            nc.psum_tensor("ps_r2", [128, 1024], F32) as ps_r2,
        ):
            with nc.Block() as blk:
                pe_rep = [0, 0]
                vnh = [0, 0]
                vnk = [[0] * KD, [0] * KD]
                rowm = rowbuf2[0:1, 0:S]
                rowr2 = rowbuf2[0:1, S:2 * S]

                def emit_rep2(t, hf):
                    ins = None
                    for nn in range(2):
                        t.matmul(ps_m2[:, nn * 512:(nn + 1) * 512], ones_sb,
                                 rowm[0:1, hf * 1024 + nn * 512: hf * 1024 + (nn + 1) * 512],
                                 start=True, stop=True)
                    for nn in range(2):
                        ins = t.matmul(ps_r2[:, nn * 512:(nn + 1) * 512], ones_sb,
                                       rowr2[0:1, hf * 1024 + nn * 512: hf * 1024 + (nn + 1) * 512],
                                       start=True, stop=True)
                    ins.then_inc(s_pe, 1)
                    n["pe"] += 1
                    pe_rep[hf] = n["pe"]

                def emit_norm2(v, hf):
                    v.wait_ge(s_pe, pe_rep[hf])
                    for k in range(KD):
                        vop(v, "tensor_sub",
                            act_bf[:, k * S + hf * 1024: k * S + hf * 1024 + 1024],
                            x2buf[:, k * S + hf * 1024: k * S + hf * 1024 + 1024],
                            ps_m2[:])
                        vnk[hf][k] = vop(v, "tensor_mul",
                            act_bf[:, k * S + hf * 1024: k * S + hf * 1024 + 1024],
                            act_bf[:, k * S + hf * 1024: k * S + hf * 1024 + 1024],
                            ps_r2[:])
                    vnh[hf] = n["v"]

                @blk.tensor
                def _(t):
                    t.wait_ge(s_v, vrows)
                    emit_rep2(t, 0)

                @blk.vector
                def _(v):
                    emit_norm2(v, 0)

                pe_rep[1] = pe_rep[0] + 1

                @blk.vector
                def _(v):
                    emit_norm2(v, 1)

                @blk.gpsimd
                def _(g):
                    for i, (lay, hf, m) in enumerate(sched):
                        if i < 3:
                            continue
                        src, kc = wsrc[lay]
                        g.wait_ge(s_mm, mm0 + i - 2)
                        g.dma_start(wb[i % 3][:, 0: kc * 128], src[m]).then_inc(swl[i % 3], 16)

                @blk.tensor
                def _(t):
                    for i, (lay, hf, m) in enumerate(sched):
                        src, kc = wsrc[lay]
                        if i == 2:
                            t.wait_ge(s_v, vnh[0])
                            emit_rep2(t, 1)
                        t.wait_ge(swl[i % 3], 16 * (i // 3 + 1))
                        if m == 0:
                            if evs_b[i] > 0:
                                t.wait_ge(s_evs, evs0 + evs_b[i])
                            if evv_b[i] > 0:
                                t.wait_ge(s_evv, evv0 + evv_b[i])
                        elif i >= 2:
                            lay2 = sched[i - 2][0]
                            if eng[lay2] == "s":
                                t.wait_ge(s_evs, evs0 + evs_b[i - 2] + 1)
                            else:
                                t.wait_ge(s_evv, evv0 + evv_b[i - 2] + 1)
                        if lay in ("w1", "w2"):
                            rhs, rb, rstride = act_bf, hf * THALF, S
                        elif lay == "h1":
                            rhs, rb, rstride = hbufB, 0, THALF
                        elif lay == "h2":
                            rhs, rb, rstride = hbufA, 0, THALF
                        else:
                            rhs, rb, rstride = hbufB, 0, THALF
                        ins = None
                        for k in range(kc):
                            if lay == "w1" and m == 0:
                                t.wait_ge(s_v, vnk[hf][k])
                            for nn in range(2):
                                ins = t.matmul(
                                    ps_f[:, (i % 2) * 1024 + nn * 512:(i % 2) * 1024 + (nn + 1) * 512],
                                    wb[i % 3][:, k * 128:(k + 1) * 128],
                                    rhs[:, k * rstride + rb + nn * 512: k * rstride + rb + (nn + 1) * 512],
                                    start=(k == 0), stop=(k == kc - 1))
                        ins.then_inc(s_mm, 1)

                n["wc"] += NS
                n["mm"] += NS

                @blk.scalar
                def _(sc):
                    sc.wait_ge(s_x2st, 16 * KD)
                    for i, (lay, hf, m) in enumerate(sched):
                        if eng[lay] != "s":
                            continue
                        sc.wait_ge(s_mm, mm0 + i + 1)
                        if lay == "w1" and hf == 0 and m == 8:
                            sc.wait_ge(s_pe, pe_rep[1])
                        if lay == "w1" and hf == 0 and m == 16:
                            sc.wait_ge(s_v, vnh[1])
                        dst = hbufA if lay in ("w1", "h1") else hbufB
                        func = AF.Silu if lay == "w1" else AF.Gelu
                        ssb = {"w1": s1_sb, "h1": sh1_sb, "h2": sh2_sb}[lay]
                        sc.activation(dst[:, m * THALF:(m + 1) * THALF],
                                      ps_f[:, (i % 2) * 1024:(i % 2) * 1024 + 1024],
                                      func, scale=ssb[:, m:m + 1]).then_inc(s_evs, 1)
                        n["evs"] += 1

                @blk.vector
                def _(v):
                    v.wait_ge(s_x2st, 16 * KD)
                    nxr = 0
                    nst = 0
                    for i, (lay, hf, m) in enumerate(sched):
                        if eng[lay] != "v":
                            continue
                        v.wait_ge(s_mm, mm0 + i + 1)
                        v.wait_ge(s_v, n["v"])
                        if lay == "w2":
                            v.scalar_tensor_tensor(
                                hbufB[:, m * THALF:(m + 1) * THALF],
                                ps_f[:, (i % 2) * 1024:(i % 2) * 1024 + 1024],
                                s2_sb[:, m:m + 1],
                                hbufA[:, m * THALF:(m + 1) * THALF],
                                ALU.mult, ALU.mult).then_inc(s_evv, 1)
                            n["evv"] += 1
                        else:
                            v.wait_ge(sxr[nxr % 2], 16 * (KD // 2) + 16 * (nxr // 2 + 1))
                            if nst >= 2:
                                v.wait_ge(sout[nst % 2], 16 * (nst // 2))
                            v.scalar_tensor_tensor(
                                ostg[:, (nst % 2) * 1024:(nst % 2) * 1024 + 1024],
                                ps_f[:, (i % 2) * 1024:(i % 2) * 1024 + 1024],
                                s3_sb[:, m:m + 1],
                                x2rb[:, (nxr % 2) * 1024:(nxr % 2) * 1024 + 1024],
                                ALU.mult, ALU.add).then_inc(s_evv, 1)
                            n["evv"] += 1
                            nxr += 1
                            nst += 1

                @blk.sync
                def _(sy):
                    nxr = 0
                    nst = 0
                    for i, (lay, hf, m) in enumerate(sched):
                        if lay != "w3":
                            continue
                        if nxr == 0:
                            sy.wait_ge(s_x2st, 16 * KD)
                        if nxr >= 2:
                            sy.wait_ge(s_evv, w3_evv_val[nxr - 2])
                        sy.dma_start(x2rb[:, (nxr % 2) * 1024:(nxr % 2) * 1024 + 1024],
                                     x2hbm[:, m * S + hf * 1024: m * S + hf * 1024 + 1024]
                                     ).then_inc(sxr[nxr % 2], 16)
                        nxr += 1
                        sy.wait_ge(s_evv, w3_evv_val[nst])
                        sy.dma_start(out_fm[:, m * S + hf * 1024: m * S + hf * 1024 + 1024],
                                     ostg[:, (nst % 2) * 1024:(nst % 2) * 1024 + 1024]
                                     ).then_inc(sout[nst % 2], 16)
                        nst += 1
                    sy.wait_ge(sout[0], 16 * 8)
                    sy.wait_ge(sout[1], 16 * 8)

    return nc


def kernel(**inputs):
    ins = inputs
    x = np.asarray(ins["L0_attn_x"], np.float32)

    def stage_w(wmat):
        WT = np.ascontiguousarray(np.asarray(wmat, np.float32).T)
        kin, kout = WT.shape
        KC, MT = kin // 128, kout // 128
        return np.ascontiguousarray(
            WT.reshape(KC, 128, MT, 128).transpose(2, 1, 0, 3).reshape(MT, 128, KC * 128))

    wq_m = np.asarray(ins["L0_attn_wq"], np.float32).transpose(1, 0, 2).reshape(D, D)
    wk_m = np.asarray(ins["L0_attn_wk"], np.float32).transpose(1, 0, 2).reshape(D, D)
    wv_m = np.asarray(ins["L0_attn_wv"], np.float32).transpose(1, 0, 2).reshape(D, D)
    wq_st = np.ascontiguousarray(
        wq_m.reshape(KD, 128, NPAIR, 128).transpose(1, 2, 0, 3).reshape(128, 8192))
    wk_st = np.ascontiguousarray(
        wk_m.reshape(KD, 128, NPAIR, 128).transpose(1, 2, 0, 3).reshape(128, 8192))
    wv_st = np.ascontiguousarray(
        wv_m.reshape(KD, 128, D).transpose(1, 0, 2).reshape(128, 8192))
    woT = np.ascontiguousarray(np.asarray(ins["L0_attn_wo"], np.float32).T)
    wo_st = np.ascontiguousarray(
        woT.reshape(KD, 128, KD, 128).transpose(1, 2, 0, 3).reshape(128, 8192))

    bq_st = np.ascontiguousarray(
        np.asarray(ins["L0_attn_bq"], np.float32).reshape(NPAIR, 2, 64)
        .transpose(1, 2, 0).reshape(128, NPAIR))
    bk_st = np.ascontiguousarray(
        np.asarray(ins["L0_attn_bk"], np.float32).reshape(NPAIR, 2, 64)
        .transpose(1, 2, 0).reshape(128, NPAIR))
    bv_st = np.ascontiguousarray(np.asarray(ins["L0_attn_bv"], np.float32).reshape(1, D))
    bo_st = np.ascontiguousarray(np.asarray(ins["L0_attn_bo"], np.float32).reshape(KD, 128).T)

    w1_st = stage_w(ins["L0_ffn_w1_w"])
    w2_st = stage_w(ins["L0_ffn_w2_w"])
    h1_st = stage_w(ins["L0_ffn_h1_w"])
    h2_st = stage_w(ins["L0_ffn_h2_w"])
    w3_st = stage_w(ins["L0_ffn_w3_w"])
    s1_st = np.ascontiguousarray(np.asarray(ins["L0_ffn_w1_s"], np.float32).reshape(KH, 128).T)
    s2_st = np.ascontiguousarray(np.asarray(ins["L0_ffn_w2_s"], np.float32).reshape(KH, 128).T)
    sh1_st = np.ascontiguousarray(np.asarray(ins["L0_ffn_h1_s"], np.float32).reshape(KH, 128).T)
    sh2_st = np.ascontiguousarray(np.asarray(ins["L0_ffn_h2_s"], np.float32).reshape(KH, 128).T)
    s3_st = np.ascontiguousarray(np.asarray(ins["L0_ffn_w3_s"], np.float32).reshape(KD, 128).T)

    nc = build_nc()
    in_maps = []
    for c in range(B):
        xc = np.ascontiguousarray(
            x[c].T.reshape(KD, 128, S).transpose(1, 0, 2).reshape(128, KD * S))
        in_maps.append({
            "xst": xc, "wq": wq_st, "wk": wk_st, "wv": wv_st, "wo": wo_st,
            "bq": bq_st, "bk": bk_st, "bvr": bv_st, "bo": bo_st,
            "w1": w1_st, "w2": w2_st, "h1": h1_st, "h2": h2_st, "w3": w3_st,
            "s1": s1_st, "s2": s2_st, "sh1": sh1_st, "sh2": sh2_st, "s3": s3_st,
        })
    res = run_bass_kernel_spmd(nc, in_maps, core_ids=list(range(B)))
    outs = []
    for c in range(B):
        o = np.asarray(res.results[c]["out_fm"], np.float32)
        outs.append(o.reshape(128, KD, S).transpose(1, 0, 2).reshape(D, S).T)
    return np.ascontiguousarray(np.stack(outs, 0))


# version 26
# speedup vs baseline: 1.0822x; 1.0822x over previous
import numpy as np
from contextlib import ExitStack
import concourse.bass as bass
import concourse.mybir as mybir
from concourse.bass_utils import run_bass_kernel_spmd

F32 = mybir.dt.float32
BF16 = mybir.dt.bfloat16
AF = mybir.ActivationFunctionType
ALU = mybir.AluOpType
AX = mybir.AxisListType

B, S, D, H, E, HID = 8, 2048, 1024, 16, 64, 4096
KD = D // 128
KH = HID // 128
NPAIR = H // 2
TC = S // 128
QS = float(8.0 ** -0.5)
LN_EPS = 1e-5
THALF = S // 2

AR = {
    "xbuf":    (0,      65536),
    "xsq":     (65536,  16384),
    "rowbuf1": (81920,  16384),
    "sqbf1":   (98304,  8192),
    "wvld":    (115584, 16384),
    "wkld":    (131968, 16384),
    "wqld":    (148352, 16384),
    "q_sb":    (0,      32768),
    "k_sb":    (32768,  32768),
    "vt_sb":   (65536,  33280),
    "bvrep":   (98816,  4096),
    "selpair": (98816,  4096),
    "q2buf":   (102912, 4096),
    "crow":    (107264, 128),
    "pbuf":    (107392, 8192),
    "dtmp":    (115584, 8192),
    "denom":   (123776, 8192),
    "wo_sb":   (148352, 16384),
    "attn_n":  (0,      32768),
    "x2buf":   (32768,  65536),
    "rtile":   (107392, 8192),
    "xrb":     (131968, 16384),
    "xsq2":    (0,      16384),
    "rowbuf2": (16384,  16384),
    "sqbf2":   (98304,  8192),
    "hbufA":   (0,      65536),
    "hbufB":   (65536,  65536),
    "x2rb":    (131072, 8192),
    "ostg":    (139264, 8192),
    "wb0":     (148352, 8192),
    "wb1":     (156544, 8192),
    "wb2":     (164736, 8192),
}
ARENA_SIZE = 172928


def build_nc():
    nc = bass.Bass("TRN2", target_bir_lowering=False)

    xst = nc.dram_tensor("xst", [128, KD * S], F32, kind="ExternalInput").ap()
    wq_d = nc.dram_tensor("wq", [128, 8192], F32, kind="ExternalInput").ap()
    wk_d = nc.dram_tensor("wk", [128, 8192], F32, kind="ExternalInput").ap()
    wv_d = nc.dram_tensor("wv", [128, 8192], F32, kind="ExternalInput").ap()
    wo_d = nc.dram_tensor("wo", [128, 8192], F32, kind="ExternalInput").ap()
    bq_d = nc.dram_tensor("bq", [128, 8], F32, kind="ExternalInput").ap()
    bk_d = nc.dram_tensor("bk", [128, 8], F32, kind="ExternalInput").ap()
    bv_d = nc.dram_tensor("bvr", [1, 1024], F32, kind="ExternalInput").ap()
    bo_d = nc.dram_tensor("bo", [128, 8], F32, kind="ExternalInput").ap()
    w1_d = nc.dram_tensor("w1", [KH, 128, KD * 128], F32, kind="ExternalInput").ap()
    w2_d = nc.dram_tensor("w2", [KH, 128, KD * 128], F32, kind="ExternalInput").ap()
    h1_d = nc.dram_tensor("h1", [KH, 128, KH * 128], F32, kind="ExternalInput").ap()
    h2_d = nc.dram_tensor("h2", [KH, 128, KH * 128], F32, kind="ExternalInput").ap()
    w3_d = nc.dram_tensor("w3", [KD, 128, KH * 128], F32, kind="ExternalInput").ap()
    s1_d = nc.dram_tensor("s1", [128, KH], F32, kind="ExternalInput").ap()
    s2_d = nc.dram_tensor("s2", [128, KH], F32, kind="ExternalInput").ap()
    sh1_d = nc.dram_tensor("sh1", [128, KH], F32, kind="ExternalInput").ap()
    sh2_d = nc.dram_tensor("sh2", [128, KH], F32, kind="ExternalInput").ap()
    s3_d = nc.dram_tensor("s3", [128, KD], F32, kind="ExternalInput").ap()
    out_fm = nc.dram_tensor("out_fm", [128, KD * S], F32, kind="ExternalOutput").ap()
    x2hbm = nc.dram_tensor("x2hbm", [128, KD * S], F32, kind="Internal").ap()

    ones_c = nc.inline_tensor(np.ones((1, 128), np.float32), "ones_c").ap()
    onescol_c = nc.inline_tensor(np.ones((128, 1), np.float32), "onescol_c").ap()
    sp = np.zeros((16, NPAIR * 128), np.float32)
    for j in range(NPAIR):
        sp[2 * j, j * 128: j * 128 + 64] = 1.0
        sp[2 * j + 1, j * 128 + 64: j * 128 + 128] = 1.0
    selpair_c = nc.inline_tensor(sp, "selpair_c").ap()

    ones_sb = nc.alloc_sbuf_tensor("ones_sb", [1, 128], F32)[:]
    onescol_sb = nc.alloc_sbuf_tensor("onescol_sb", [128, 1], F32)[:]
    onescol_bf = nc.alloc_sbuf_tensor("onescol_bf", [128, 1], BF16)[:]
    bqs = nc.alloc_sbuf_tensor("bqs", [128, 8], F32)[:]
    bks = nc.alloc_sbuf_tensor("bks", [128, 8], F32)[:]
    bo_sb = nc.alloc_sbuf_tensor("bo_sb", [128, 8], F32)[:]
    bvr_sb = nc.alloc_sbuf_tensor("bvr_sb", [1, 1024], F32)[:]
    s1_sb = nc.alloc_sbuf_tensor("s1_sb", [128, KH], F32)[:]
    s2_sb = nc.alloc_sbuf_tensor("s2_sb", [128, KH], F32)[:]
    sh1_sb = nc.alloc_sbuf_tensor("sh1_sb", [128, KH], F32)[:]
    sh2_sb = nc.alloc_sbuf_tensor("sh2_sb", [128, KH], F32)[:]
    s3_sb = nc.alloc_sbuf_tensor("s3_sb", [128, KD], F32)[:]
    cneg = nc.alloc_sbuf_tensor("cneg", [128, NPAIR], F32)[:]
    act_bf = nc.alloc_sbuf_tensor("act_bf", [128, KD * S], BF16)[:]

    arena_base = (nc.sbuf_base + 255) & ~255
    assert arena_base + ARENA_SIZE <= nc.sbuf_top, (
        f"arena overflow: base={arena_base} size={ARENA_SIZE} top={nc.sbuf_top}")

    def arena(name, shape, dtype):
        off, sz = AR[name]
        esz = 2 if dtype == BF16 else 4
        nbytes = int(np.prod(shape[1:])) * esz
        assert nbytes <= sz, (name, nbytes, sz)
        return nc.alloc_sbuf_tensor_at(name, shape, dtype,
                                       offset=arena_base + off)[:]

    xbuf = arena("xbuf", [128, KD * S], F32)
    xsq = arena("xsq", [128, 2 * S], F32)
    sqbf1 = arena("sqbf1", [128, 2 * S], BF16)
    rowbuf1 = arena("rowbuf1", [1, 2 * S], F32)
    wv_sb = arena("wvld", [128, 8192], BF16)
    wk_sb = arena("wkld", [128, 8192], BF16)
    wq_sb = arena("wqld", [128, 8192], BF16)
    q_sb = arena("q_sb", [128, NPAIR * S], BF16)
    k_sb = arena("k_sb", [128, NPAIR * S], BF16)
    vt_sb = arena("vt_sb", [128, TC * 16 * 65], BF16)
    bvrep_sb = arena("bvrep", [128, 1024], F32)
    q2buf = arena("q2buf", [128, S], BF16)
    crow = arena("crow", [1, 32], F32)
    denom_sb = arena("denom", [16, S], F32)
    selpair_sb = arena("selpair", [16, NPAIR * 128], BF16)
    pbuf = arena("pbuf", [128, 2 * S], BF16)
    dtmp = arena("dtmp", [65, 2 * 1024], F32)
    wo_sb = arena("wo_sb", [128, 8192], BF16)
    attn_n = arena("attn_n", [128, NPAIR * S], BF16)
    x2buf = arena("x2buf", [128, KD * S], F32)
    rtile = arena("rtile", [16, S], BF16)
    xrb = arena("xrb", [128, 2 * S], F32)
    xsq2 = arena("xsq2", [128, 2 * S], F32)
    sqbf2 = arena("sqbf2", [128, 2 * S], BF16)
    rowbuf2 = arena("rowbuf2", [1, 2 * S], F32)
    hbufA = arena("hbufA", [128, KH * THALF], BF16)
    hbufB = arena("hbufB", [128, KH * THALF], BF16)
    wb0 = arena("wb0", [128, KH * 128], BF16)
    wb1 = arena("wb1", [128, KH * 128], BF16)
    wb2 = arena("wb2", [128, KH * 128], BF16)
    x2rb = arena("x2rb", [128, 2 * 1024], F32)
    ostg = arena("ostg", [128, 2 * 1024], F32)

    n = {"v": 0, "sc": 0, "pe": 0, "exp": 0, "att": 0, "scx": 0, "dcp": 0,
         "wc": 0, "mm": 0, "evs": 0, "evv": 0}

    es = ExitStack()
    with es:
        s_x = es.enter_context(nc.semaphore("s_x"))
        s_const = es.enter_context(nc.semaphore("s_const"))
        s_wqkv = es.enter_context(nc.semaphore("s_wqkv"))
        s_small = es.enter_context(nc.semaphore("s_small"))
        s_v = es.enter_context(nc.semaphore("s_v"))
        s_sc = es.enter_context(nc.semaphore("s_sc"))
        s_pe = es.enter_context(nc.semaphore("s_pe"))
        s_exp = es.enter_context(nc.semaphore("s_exp"))
        s_att = es.enter_context(nc.semaphore("s_att"))
        s_scx = es.enter_context(nc.semaphore("s_scx"))
        s_dcp = es.enter_context(nc.semaphore("s_dcp"))
        s_wl0 = es.enter_context(nc.semaphore("s_wl0"))
        s_wl1 = es.enter_context(nc.semaphore("s_wl1"))
        s_wl2 = es.enter_context(nc.semaphore("s_wl2"))
        s_wc = es.enter_context(nc.semaphore("s_wc"))
        s_mm = es.enter_context(nc.semaphore("s_mm"))
        s_evs = es.enter_context(nc.semaphore("s_evs"))
        s_evv = es.enter_context(nc.semaphore("s_evv"))
        s_xr0 = es.enter_context(nc.semaphore("s_xr0"))
        s_xr1 = es.enter_context(nc.semaphore("s_xr1"))
        s_o0 = es.enter_context(nc.semaphore("s_o0"))
        s_o1 = es.enter_context(nc.semaphore("s_o1"))
        s_x2st = es.enter_context(nc.semaphore("s_x2st"))
        s_dd0 = es.enter_context(nc.semaphore("s_dd0"))
        s_vt = es.enter_context(nc.semaphore("s_vt"))
        s_qk = es.enter_context(nc.semaphore("s_qk"))
        s_cg = es.enter_context(nc.semaphore("s_cg"))
        s_xe = es.enter_context(nc.semaphore("s_xe"))
        s_dd1 = es.enter_context(nc.semaphore("s_dd1"))

        swl = [s_wl0, s_wl1, s_wl2]
        sxr = [s_xr0, s_xr1]
        sout = [s_o0, s_o1]
        sdd = [s_dd0, s_dd1]

        def vop(v, meth, *a, **kw):
            v.wait_ge(s_v, n["v"])
            getattr(v, meth)(*a, **kw).then_inc(s_v, 1)
            n["v"] += 1
            return n["v"]

        def scop(sc, meth, *a, **kw):
            sc.wait_ge(s_sc, n["sc"])
            getattr(sc, meth)(*a, **kw).then_inc(s_sc, 1)
            n["sc"] += 1
            return n["sc"]

        def emit_layernorm(src, xsqb, sqbf, rowb, tag, normalize=True):
            rowm = rowb[0:1, 0:S]
            rowr = rowb[0:1, S:2 * S]
            rowt = xsqb[0:1, 0:S]
            rowu = xsqb[0:1, S:2 * S]
            with (
                nc.psum_tensor("ps_sx" + tag, [1, S], F32) as ps_sx,
                nc.psum_tensor("ps_sq" + tag, [1, S], F32) as ps_sq,
            ):
                with nc.Block() as blk:
                    pe0ln = n["pe"]

                    @blk.scalar
                    def _(sc):
                        for k in range(KD):
                            if k >= 2:
                                sc.wait_ge(s_pe, pe0ln + k - 1)
                            scop(sc, "activation", sqbf[:, (k % 2) * S:(k % 2) * S + S],
                                 src[:, k * S:(k + 1) * S], AF.Square)

                    sc0 = n["sc"] - KD

                    @blk.tensor
                    def _(t):
                        t.wait_ge(s_const, 32)
                        t.wait_ge(s_small, 16 * 10)
                        for k in range(KD):
                            for nn in range(4):
                                t.matmul(ps_sx[:, nn * 512:(nn + 1) * 512], onescol_sb,
                                         src[:, k * S + nn * 512: k * S + (nn + 1) * 512],
                                         start=(k == 0), stop=(k == KD - 1))
                            t.wait_ge(s_sc, sc0 + k + 1)
                            ins = None
                            for nn in range(4):
                                ins = t.matmul(
                                    ps_sq[:, nn * 512:(nn + 1) * 512], onescol_bf,
                                    sqbf[:, (k % 2) * S + nn * 512: (k % 2) * S + (nn + 1) * 512],
                                    start=(k == 0), stop=(k == KD - 1))
                            ins.then_inc(s_pe, 1)

                    n["pe"] += KD

                    @blk.vector
                    def _(v):
                        v.wait_ge(s_pe, n["pe"])
                        vop(v, "tensor_scalar_mul", rowm, ps_sx[:], 1.0 / D)
                        vop(v, "tensor_mul", rowu, rowm, rowm)
                        vop(v, "scalar_tensor_tensor", rowr, ps_sq[:], 1.0 / D,
                            rowu, ALU.mult, ALU.subtract)
                        vop(v, "tensor_scalar_add", rowu, rowr, LN_EPS)

                    @blk.scalar
                    def _(sc):
                        sc.wait_ge(s_v, n["v"])
                        scop(sc, "activation", rowt, rowu, AF.Ln)
                        scop(sc, "activation", rowt, rowt, AF.Exp, scale=-0.5)

                    @blk.vector
                    def _(v):
                        v.wait_ge(s_sc, n["sc"])
                        vop(v, "tensor_mul", rowr, rowt, rowt)
                        vop(v, "tensor_mul", rowr, rowr, rowu)
                        vop(v, "tensor_scalar", rowr, rowr, -0.5, 1.5,
                            ALU.mult, ALU.add)
                        vop(v, "tensor_mul", rowr, rowt, rowr)

            if not normalize:
                return n["v"]
            with (
                nc.psum_tensor("ps_mrep" + tag, [128, S], F32) as ps_mrep,
                nc.psum_tensor("ps_rrep" + tag, [128, S], F32) as ps_rrep,
            ):
                with nc.Block() as blk:
                    @blk.tensor
                    def _(t):
                        ins = None
                        for nn in range(4):
                            t.matmul(ps_mrep[:, nn * 512:(nn + 1) * 512], ones_sb,
                                     rowm[0:1, nn * 512:(nn + 1) * 512],
                                     start=True, stop=True)
                        for nn in range(4):
                            ins = t.matmul(ps_rrep[:, nn * 512:(nn + 1) * 512], ones_sb,
                                           rowr[0:1, nn * 512:(nn + 1) * 512],
                                           start=True, stop=True)
                        ins.then_inc(s_pe, 1)

                    n["pe"] += 1

                    @blk.vector
                    def _(v):
                        v.wait_ge(s_pe, n["pe"])
                        for k in range(KD):
                            vop(v, "tensor_sub", xsqb[:, (k % 2) * S:(k % 2) * S + S],
                                src[:, k * S:(k + 1) * S], ps_mrep[:])
                            vop(v, "tensor_mul", act_bf[:, k * S:(k + 1) * S],
                                xsqb[:, (k % 2) * S:(k % 2) * S + S], ps_rrep[:])

        with nc.Block() as blk:
            @blk.gpsimd
            def _(g):
                g.dma_start(onescol_bf, onescol_c).then_inc(s_small, 16)
                g.dma_start(bqs, bq_d).then_inc(s_small, 16)
                g.dma_start(bks, bk_d).then_inc(s_small, 16)
                g.dma_start(bo_sb, bo_d).then_inc(s_small, 16)
                g.dma_start(bvr_sb, bv_d).then_inc(s_small, 16)
                g.dma_start(s1_sb, s1_d).then_inc(s_small, 16)
                g.dma_start(s2_sb, s2_d).then_inc(s_small, 16)
                g.dma_start(sh1_sb, sh1_d).then_inc(s_small, 16)
                g.dma_start(sh2_sb, sh2_d).then_inc(s_small, 16)
                g.dma_start(s3_sb, s3_d).then_inc(s_small, 16)
                g.wait_ge(s_x, 16)
                g.dma_start(wq_sb, wq_d).then_inc(s_wqkv, 16)
                g.dma_start(wk_sb, wk_d).then_inc(s_wqkv, 16)
                g.dma_start(wv_sb, wv_d).then_inc(s_wqkv, 16)

            @blk.sync
            def _(sy):
                sy.dma_start(xbuf, xst).then_inc(s_x, 16)
                sy.dma_start(ones_sb, ones_c).then_inc(s_const, 16)
                sy.dma_start(onescol_sb, onescol_c).then_inc(s_const, 16)

            @blk.vector
            def _(v):
                v.wait_ge(s_x, 16)
                v.wait_ge(s_small, 16 * 10)
                vop(v, "tensor_scalar_mul", bqs, bqs, QS)
                vop(v, "tensor_scalar_mul", bks, bks, QS)

            @blk.tensor
            def _(t):
                t.wait_ge(s_x, 16)

        vrows1 = emit_layernorm(xbuf, xsq, sqbf1, rowbuf1, "a", normalize=False)

        TQ = S // 4
        vt3 = vt_sb.rearrange("p (s e) -> p s e", e=65)
        ILIST = [(j, qq) for j in range(NPAIR) for qq in range(4)]
        NI = len(ILIST)
        exp0 = n["exp"]
        att0 = n["att"]
        scx0 = n["scx"]
        dcp0 = n["dcp"]
        fence_vt = crow[0:1, 8:9]
        fence_qk = crow[0:1, 11:12]
        fence_cg = crow[0:1, 12:13]

        with nc.psum_tensor("ps_bv", [128, 1024], F32) as ps_bv:
            with nc.Block() as blk:
                @blk.tensor
                def _(t):
                    t.wait_ge(s_small, 16 * 10)
                    ins = None
                    for nn in range(2):
                        ins = t.matmul(ps_bv[:, nn * 512:(nn + 1) * 512], ones_sb,
                                       bvr_sb[0:1, nn * 512:(nn + 1) * 512],
                                       start=True, stop=True)
                    ins.then_inc(s_pe, 1)

                n["pe"] += 1

                @blk.vector
                def _(v):
                    v.wait_ge(s_pe, n["pe"])
                    vop(v, "tensor_copy", bvrep_sb, ps_bv[:])

        with (
            nc.psum_tensor("ps_sc", [128, 2 * 1024], F32) as ps_sc_g,
            nc.psum_tensor("ps_aE", [65, TQ], F32) as ps_aE_g,
            nc.psum_tensor("ps_aO", [65, TQ], F32) as ps_aO_g,
            nc.psum_tensor("ps_x", [128, 1024], F32) as ps_x_g,
        ):
            ps_sc = ps_sc_g[:]
            ps_aE = ps_aE_g[:]
            ps_aO = ps_aO_g[:]
            ps_x = ps_x_g[:]

            xu = {"n": 0, "xe": 0}
            dtmp_vals = {}
            proj_pe = {}
            vch_pe = {}
            s2q_val = [None] * (NPAIR + 1)
            s2k_val = [None] * (NPAIR + 1)
            pe_knrm = [0] * (NPAIR + 1)
            pe_crep = [0] * (NPAIR + 1)

            def pe_use_slot(t):
                if xu["n"] > 0:
                    t.wait_ge(s_xe, xu["n"])

            def emit_proj(t, j, which, hh):
                wsb = wq_sb if which == "q" else wk_sb
                pe_use_slot(t)
                ins = None
                for k in range(KD):
                    for nn in range(2):
                        ins = t.matmul(
                            ps_x[:, nn * 512:(nn + 1) * 512],
                            wsb[:, j * 1024 + k * 128: j * 1024 + (k + 1) * 128],
                            act_bf[:, k * S + hh * 1024 + nn * 512: k * S + hh * 1024 + (nn + 1) * 512],
                            start=(k == 0), stop=(k == KD - 1))
                ins.then_inc(s_pe, 1)
                n["pe"] += 1
                xu["n"] += 1
                proj_pe[(j, which, hh)] = n["pe"]

            def emit_vchunk(t, c):
                pe_use_slot(t)
                ins = None
                for k in range(KD):
                    for nn in range(2):
                        ins = t.matmul(
                            ps_x[:, nn * 512:(nn + 1) * 512],
                            act_bf[:, k * S + c * 128: k * S + (c + 1) * 128],
                            wv_sb[:, k * 1024 + nn * 512: k * 1024 + (nn + 1) * 512],
                            start=(k == 0), stop=(k == KD - 1))
                ins.then_inc(s_pe, 1)
                n["pe"] += 1
                xu["n"] += 1
                vch_pe[c] = n["pe"]

            def emit_scores(t, I, c):
                j, qq = ILIST[I]
                if c == 0 and qq == 0:
                    t.wait_ge(s_qk, j + 1)
                if I == 0 and c <= 1:
                    t.wait_ge(s_vt, TC)
                wv_ = exp0 + I * TC + c - 1
                if wv_ > 0:
                    t.wait_ge(s_exp, wv_)
                bufo = (c % 2) * 1024
                ins = None
                for hh in range(2):
                    p0 = hh * 64
                    ins = t.matmul(
                        ps_sc[:, bufo + hh * TQ: bufo + (hh + 1) * TQ],
                        k_sb[p0:p0 + 64, j * S + c * 128: j * S + (c + 1) * 128],
                        q_sb[p0:p0 + 64, j * S + qq * TQ: j * S + (qq + 1) * TQ],
                        start=True, stop=True)
                ins.then_inc(s_scx, 1)

            def emit_att(t, I, c, exp_wait=False):
                j, qq = ILIST[I]
                if exp_wait:
                    t.wait_ge(s_exp, exp0 + I * TC + c + 1)
                if c == 0 and I >= 1:
                    t.wait_ge(s_dcp, dcp0 + I)
                if I == 0:
                    t.wait_ge(s_vt, c + 1)
                elif I == 1 and c == 0:
                    t.wait_ge(s_vt, TC)
                sb = (c % 4) * 1024
                ins = None
                for hh, ps_a in ((0, ps_aE), (1, ps_aO)):
                    vslot = (c * 16 + 2 * j + hh) * 65
                    ins = t.matmul(
                        ps_a[:, :],
                        vt_sb[:, vslot: vslot + 65],
                        pbuf[:, sb + hh * TQ: sb + (hh + 1) * TQ],
                        start=(c == 0), stop=(c == TC - 1))
                ins.then_inc(s_att, 1)

            n["scx"] += NI * TC
            n["att"] += NI * TC
            n["exp"] += NI * TC

            with nc.Block() as blk:
                def emit_evac_one(v, j, w, hh):
                    v.wait_ge(s_pe, proj_pe[(j, w, hh)])
                    dst = q_sb if w == "q" else k_sb
                    bias = bqs if w == "q" else bks
                    v.wait_ge(s_v, n["v"])
                    v.tensor_scalar(
                        dst[:, j * S + hh * 1024: j * S + (hh + 1) * 1024],
                        ps_x[:], QS, bias[:, j:j + 1],
                        ALU.mult, ALU.add).then_inc(s_xe, 1)
                    xu["xe"] += 1

                def emit_qk_tail(v, j):
                    v.wait_ge(s_xe, xu["xe"])
                    if j >= 1:
                        v.wait_ge(s_qk, j)
                    v.wait_ge(s_v, n["v"])
                    v.tensor_copy(fence_qk, k_sb[0:1, j * S: j * S + 1]).then_inc(s_qk, 1)
                    if j >= 1:
                        v.wait_ge(s_pe, pe_knrm[j - 1])
                    s2q_val[j] = vop(v, "tensor_mul", q2buf,
                                     q_sb[:, j * S:(j + 1) * S],
                                     q_sb[:, j * S:(j + 1) * S])

                def emit_norm(t, j, th, sval):
                    t.wait_ge(s_v, sval)
                    pe_use_slot(t)
                    ins = None
                    for nn in range(2):
                        ins = t.matmul(
                            ps_x[0:1, nn * 512:(nn + 1) * 512], onescol_bf,
                            q2buf[:, th * 1024 + nn * 512: th * 1024 + (nn + 1) * 512],
                            start=True, stop=True)
                    ins.then_inc(s_pe, 1)
                    n["pe"] += 1
                    xu["n"] += 1
                    return n["pe"]

                def v_reduce(v, pe_val, col):
                    v.wait_ge(s_pe, pe_val)
                    v.wait_ge(s_xe, xu["xe"])
                    v.wait_ge(s_v, n["v"])
                    v.reduce_max(crow[0:1, col:col + 1], ps_x[0:1, 0:1024],
                                 axis=AX.X).then_inc(s_xe, 1)
                    xu["xe"] += 1

                def emit_crep_T(t, j):
                    t.wait_ge(s_v, cscale_val[j])
                    pe_use_slot(t)
                    t.matmul(ps_x[:, 0:1], ones_sb, crow[0:1, 4:5],
                             start=True, stop=True).then_inc(s_pe, 1)
                    n["pe"] += 1
                    xu["n"] += 1
                    pe_crep[j] = n["pe"]

                def emit_cneg_V(v, j):
                    v.wait_ge(s_pe, pe_crep[j])
                    v.wait_ge(s_xe, xu["xe"])
                    v.wait_ge(s_v, n["v"])
                    v.tensor_copy(cneg[:, j:j + 1],
                                  ps_x[:, 0:1]).then_inc(s_xe, 1)
                    xu["xe"] += 1
                    v.wait_ge(s_xe, xu["xe"])
                    if j >= 1:
                        v.wait_ge(s_cg, j)
                    v.wait_ge(s_v, n["v"])
                    v.tensor_copy(fence_cg, cneg[0:1, j:j + 1]).then_inc(s_cg, 1)

                qn_pe = {}
                kn_pe = {}
                cscale_val = [None] * (NPAIR + 1)

                def c_qn_V(v, j):
                    v_reduce(v, qn_pe[(j, 0)], 9)
                    v_reduce(v, qn_pe[(j, 1)], 10)
                    v.wait_ge(s_xe, xu["xe"])
                    vop(v, "tensor_max", crow[0:1, 5:6],
                        crow[0:1, 9:10], crow[0:1, 10:11])
                    v.wait_ge(s_pe, qn_pe[(j, 1)])
                    s2k_val[j] = vop(v, "tensor_mul", q2buf,
                                     k_sb[:, j * S:(j + 1) * S],
                                     k_sb[:, j * S:(j + 1) * S])

                def c_kn_V(v, j):
                    v_reduce(v, kn_pe[(j, 0)], 9)
                    v_reduce(v, kn_pe[(j, 1)], 10)
                    v.wait_ge(s_xe, xu["xe"])
                    vop(v, "tensor_max", crow[0:1, 6:7],
                        crow[0:1, 9:10], crow[0:1, 10:11])
                    if j >= 1:
                        v.wait_ge(s_pe, pe_crep[j - 1])
                    vop(v, "tensor_add", crow[0:1, 4:5],
                        crow[0:1, 5:6], crow[0:1, 6:7])
                    cscale_val[j] = vop(v, "tensor_scalar_mul", crow[0:1, 4:5],
                                        crow[0:1, 4:5], -0.505)

                rowm1 = rowbuf1[0:1, 0:S]
                rowr1 = rowbuf1[0:1, S:2 * S]
                vn1h = [0, 0]
                relcell = crow[0:1, 13:14]

                def emit_rep1(t, row, hf):
                    pe_use_slot(t)
                    ins = None
                    for nn in range(2):
                        ins = t.matmul(
                            ps_x[:, nn * 512:(nn + 1) * 512], ones_sb,
                            row[0:1, hf * 1024 + nn * 512: hf * 1024 + (nn + 1) * 512],
                            start=True, stop=True)
                    ins.then_inc(s_pe, 1)
                    n["pe"] += 1
                    xu["n"] += 1
                    return n["pe"]

                def slot_release(v):
                    v.wait_ge(s_xe, xu["xe"])
                    v.wait_ge(s_v, n["v"])
                    v.tensor_copy(relcell, bvr_sb[0:1, 0:1]).then_inc(s_xe, 1)
                    xu["xe"] += 1

                def emit_vchunk2(t, c):
                    if c >= 2:
                        t.wait_ge(s_vt, c - 1)
                    ins = None
                    for k in range(KD):
                        for nn in range(2):
                            ins = t.matmul(
                                ps_sc[:, (c % 2) * 1024 + nn * 512:
                                      (c % 2) * 1024 + (nn + 1) * 512],
                                act_bf[:, k * S + c * 128: k * S + (c + 1) * 128],
                                wv_sb[:, k * 1024 + nn * 512: k * 1024 + (nn + 1) * 512],
                                start=(k == 0), stop=(k == KD - 1))
                    ins.then_inc(s_pe, 1)
                    n["pe"] += 1
                    vch_pe[c] = n["pe"]

                def emit_vt_evac(v, c):
                    v.wait_ge(s_pe, vch_pe[c])
                    v.wait_ge(s_v, n["v"])
                    v.tensor_add(
                        vt3[:, c * 16:(c + 1) * 16, 0:64],
                        ps_sc[:, (c % 2) * 1024:(c % 2) * 1024 + 1024]
                        .rearrange("p (h e) -> p h e", e=64),
                        bvrep_sb.rearrange("p (h e) -> p h e", e=64)
                    ).then_inc(s_v, 1)
                    n["v"] += 1
                    v.wait_ge(s_v, n["v"])
                    if c >= 1:
                        v.wait_ge(s_vt, c)
                    v.tensor_copy(
                        fence_vt, vt_sb[0:1, c * 16 * 65: c * 16 * 65 + 1]
                    ).then_inc(s_vt, 1)

                @blk.tensor
                def _(t):
                    t.wait_ge(s_v, vrows1)
                    emit_rep1(t, rowm1, 0)

                @blk.vector
                def _(v, pv=n["pe"]):
                    vop(v, "memset", vt3[:, 0:112, 64:65], 1.0)
                    v.wait_ge(s_pe, pv)
                    for k in range(KD):
                        vop(v, "tensor_sub", act_bf[:, k * S: k * S + 1024],
                            xbuf[:, k * S: k * S + 1024], ps_x[:])
                    slot_release(v)

                @blk.tensor
                def _(t):
                    emit_rep1(t, rowr1, 0)

                @blk.vector
                def _(v, pv=n["pe"]):
                    v.wait_ge(s_pe, pv)
                    for k in range(KD):
                        vop(v, "tensor_mul", act_bf[:, k * S: k * S + 1024],
                            act_bf[:, k * S: k * S + 1024], ps_x[:])
                    vn1h[0] = n["v"]
                    slot_release(v)

                @blk.tensor
                def _(t):
                    emit_rep1(t, rowm1, 1)

                @blk.vector
                def _(v, pv=n["pe"]):
                    v.wait_ge(s_pe, pv)
                    for k in range(KD):
                        vop(v, "tensor_sub",
                            act_bf[:, k * S + 1024: k * S + 2048],
                            xbuf[:, k * S + 1024: k * S + 2048], ps_x[:])
                    slot_release(v)

                @blk.tensor
                def _(t):
                    t.wait_ge(s_wqkv, 16 * 3)
                    t.wait_ge(s_v, vn1h[0])
                    emit_proj(t, 0, "q", 0)
                    emit_proj(t, 0, "k", 0)
                    for c in range(KD - 1):
                        emit_vchunk2(t, c)

                @blk.vector
                def _(v):
                    emit_evac_one(v, 0, "q", 0)
                    emit_evac_one(v, 0, "k", 0)
                    for c in range(KD - 1):
                        emit_vt_evac(v, c)

                @blk.tensor
                def _(t):
                    emit_vchunk2(t, KD - 1)
                    pe_r1 = emit_rep1(t, rowr1, 1)

                @blk.vector
                def _(v):
                    emit_vt_evac(v, KD - 1)
                    v.wait_ge(s_pe, n["pe"])
                    v.wait_ge(s_v, n["v"])
                    v.memset(vt3[:, 112:256, 64:65], 1.0).then_inc(s_v, 1)
                    n["v"] += 1
                    for k in range(KD):
                        vop(v, "tensor_mul",
                            act_bf[:, k * S + 1024: k * S + 2048],
                            act_bf[:, k * S + 1024: k * S + 2048], ps_x[:])
                    vn1h[1] = n["v"]
                    slot_release(v)

                @blk.tensor
                def _(t):
                    t.wait_ge(s_v, vn1h[1])
                    emit_proj(t, 0, "q", 1)
                    emit_proj(t, 0, "k", 1)
                    for c in range(KD, TC):
                        emit_vchunk2(t, c)

                @blk.vector
                def _(v):
                    emit_evac_one(v, 0, "q", 1)
                    emit_evac_one(v, 0, "k", 1)
                    emit_qk_tail(v, 0)
                    for c in range(KD, TC):
                        emit_vt_evac(v, c)

                @blk.tensor
                def _(t):
                    qn_pe[(0, 0)] = emit_norm(t, 0, 0, s2q_val[0])
                    qn_pe[(0, 1)] = emit_norm(t, 0, 1, s2q_val[0])

                @blk.vector
                def _(v):
                    c_qn_V(v, 0)

                @blk.tensor
                def _(t):
                    kn_pe[(0, 0)] = emit_norm(t, 0, 0, s2k_val[0])
                    kn_pe[(0, 1)] = emit_norm(t, 0, 1, s2k_val[0])
                    pe_knrm[0] = n["pe"]

                @blk.vector
                def _(v):
                    c_kn_V(v, 0)

                @blk.tensor
                def _(t):
                    emit_crep_T(t, 0)

                @blk.vector
                def _(v):
                    emit_cneg_V(v, 0)

                for j in range(NPAIR):
                    I0 = 4 * j
                    jn = j + 1
                    have_next = jn < NPAIR

                    if j == NPAIR - 1:
                        @blk.gpsimd
                        def _(g):
                            g.wait_ge(s_pe, proj_pe[(NPAIR - 1, "k", 1)])
                            g.dma_start(wo_sb, wo_d).then_inc(s_wqkv, 16)

                    for qq in range(4):
                        I = I0 + qq

                        @blk.tensor
                        def _(t, j=j, qq=qq, I=I, jn=jn, have_next=have_next):
                            side = {}
                            if qq == 0 and have_next:
                                for gi, (w, hh) in enumerate(
                                        (("q", 0), ("q", 1), ("k", 0), ("k", 1))):
                                    side.setdefault(gi * 4, []).append(
                                        ("proj", jn, w, hh))
                            elif qq == 1 and have_next:
                                side[1] = [("qn", jn, 0)]
                                side[3] = [("qn", jn, 1)]
                            elif qq == 2 and have_next:
                                side[1] = [("kn", jn, 0)]
                                side[3] = [("kn", jn, 1)]
                            elif qq == 3 and have_next:
                                side[1] = [("cr", jn)]
                            for c in range(TC):
                                emit_scores(t, I, c)
                                if c <= 1 and I >= 1:
                                    emit_att(t, I - 1, TC - 2 + c)
                                if c >= 2:
                                    emit_att(t, I, c - 2)
                                for item in side.get(c, ()):
                                    kind = item[0]
                                    if kind == "v":
                                        emit_vchunk(t, item[1])
                                    elif kind == "proj":
                                        emit_proj(t, item[1], item[2], item[3])
                                    elif kind == "qn":
                                        qn_pe[(item[1], item[2])] = emit_norm(
                                            t, item[1], item[2], s2q_val[item[1]])
                                    elif kind == "kn":
                                        kn_pe[(item[1], item[2])] = emit_norm(
                                            t, item[1], item[2], s2k_val[item[1]])
                                        if item[2] == 1:
                                            pe_knrm[item[1]] = n["pe"]
                                    else:
                                        emit_crep_T(t, item[1])
                            if I == NI - 1:
                                emit_att(t, I, TC - 2, exp_wait=True)
                                emit_att(t, I, TC - 1, exp_wait=True)

                        @blk.vector
                        def _(v, j=j, qq=qq, I=I, jn=jn, have_next=have_next):
                            if qq == 0 and have_next:
                                for w, hh in (("q", 0), ("q", 1),
                                              ("k", 0), ("k", 1)):
                                    emit_evac_one(v, jn, w, hh)
                                emit_qk_tail(v, jn)
                            elif qq == 1 and have_next:
                                c_qn_V(v, jn)
                            elif qq == 2 and have_next:
                                c_kn_V(v, jn)
                            elif qq == 3 and have_next:
                                emit_cneg_V(v, jn)
                            jj, qqq = ILIST[I]
                            v.wait_ge(s_att, att0 + (I + 1) * TC)
                            for kk, ps_a in ((2 * I, ps_aE), (2 * I + 1, ps_aO)):
                                if kk >= 2:
                                    v.wait_ge(sdd[kk % 2], 16 * (kk // 2))
                                dtmp_vals[kk] = vop(
                                    v, "tensor_copy",
                                    dtmp[64:65, (kk % 2) * 1024:(kk % 2) * 1024 + TQ],
                                    ps_a[64:65, :])
                            vop(v, "tensor_copy",
                                q_sb[0:64, jj * S + qqq * TQ: jj * S + (qqq + 1) * TQ],
                                ps_aE[0:64, :])
                            v.wait_ge(s_v, n["v"])
                            v.tensor_copy(
                                q_sb[64:128, jj * S + qqq * TQ: jj * S + (qqq + 1) * TQ],
                                ps_aO[0:64, :]).then_inc(s_dcp, 1)
                            n["dcp"] += 1

                    @blk.scalar
                    def _(sc, j=j, I0=I0):
                        for qq in range(4):
                            I = I0 + qq
                            for c in range(TC):
                                if qq == 0 and c == 0:
                                    sc.wait_ge(s_cg, j + 1)
                                sc.wait_ge(s_scx, scx0 + I * TC + c + 1)
                                wv_ = att0 + I * TC + c - 3
                                if wv_ > 0:
                                    sc.wait_ge(s_att, wv_)
                                sc.activation(
                                    pbuf[:, (c % 4) * 1024:(c % 4) * 1024 + 1024],
                                    ps_sc[:, (c % 2) * 1024:(c % 2) * 1024 + 1024],
                                    AF.Exp,
                                    bias=cneg[:, j:j + 1]).then_inc(s_exp, 1)

                    if j == 0:
                        @blk.gpsimd
                        def _(g):
                            g.wait_ge(s_vt, TC)
                            g.dma_start(selpair_sb, selpair_c).then_inc(s_small, 16)

                    @blk.sync
                    def _(sy, j=j, I0=I0):
                        for qq in range(4):
                            I = I0 + qq
                            jj, qqq = ILIST[I]
                            for kk, row in ((2 * I, 2 * jj), (2 * I + 1, 2 * jj + 1)):
                                sy.wait_ge(s_v, dtmp_vals[kk])
                                if kk >= 2:
                                    sy.wait_ge(sdd[kk % 2], 16 * (kk // 2))
                                sy.dma_start(
                                    denom_sb[row:row + 1, qqq * TQ:(qqq + 1) * TQ],
                                    dtmp[64:65, (kk % 2) * 1024:(kk % 2) * 1024 + TQ]
                                ).then_inc(sdd[kk % 2], 16)

        with nc.psum_tensor("ps_rep", [128, 2 * S], F32) as ps_rep:
            with nc.Block() as blk:
                @blk.vector
                def _(v):
                    v.wait_ge(s_dd0, 16 * 32)
                    v.wait_ge(s_dd1, 16 * 32)
                    vop(v, "reciprocal", denom_sb, denom_sb)
                    vop(v, "tensor_copy", rtile, denom_sb)

                v0 = n["v"]
                pe0 = n["pe"]

                @blk.tensor
                def _(t):
                    t.wait_ge(s_v, v0)
                    t.wait_ge(s_small, 16 * 11)
                    for j in range(NPAIR):
                        if j >= 2:
                            t.wait_ge(s_v, v0 + j - 1)
                        ins = None
                        for nn in range(4):
                            ins = t.matmul(
                                ps_rep[:, (j % 2) * S + nn * 512:(j % 2) * S + (nn + 1) * 512],
                                selpair_sb[:, j * 128:(j + 1) * 128],
                                rtile[:, nn * 512:(nn + 1) * 512],
                                start=True, stop=True)
                        ins.then_inc(s_pe, 1)

                n["pe"] += NPAIR

                @blk.vector
                def _(v):
                    for j in range(NPAIR):
                        v.wait_ge(s_pe, pe0 + j + 1)
                        vop(v, "tensor_mul", attn_n[:, j * S:(j + 1) * S],
                            q_sb[:, j * S:(j + 1) * S],
                            ps_rep[:, (j % 2) * S:(j % 2) * S + S])

        with nc.psum_tensor("ps_stats", [33, S], F32) as ps_stats:
            with nc.psum_tensor("ps_wo", [128, 2 * 1024], F32) as ps_wo:
                with nc.Block() as blk:
                    v0 = n["v"]
                    pe0 = n["pe"]
                    sq0 = n["sc"]
                    sqstat_pe = {}

                    @blk.gpsimd
                    def _(g):
                        g.wait_ge(s_pe, pe0 + 26)
                        g.dma_start(wb0[:, 0: KD * 128], w1_d[0]).then_inc(s_wl0, 16)
                        g.dma_start(wb1[:, 0: KD * 128], w1_d[1]).then_inc(s_wl1, 16)
                        g.dma_start(wb2[:, 0: KD * 128], w1_d[2]).then_inc(s_wl2, 16)

                    @blk.sync
                    def _(sy):
                        for m in range(KD):
                            if m >= 2:
                                sy.wait_ge(s_v, v0 + 2 * (m - 1))
                            sy.dma_start(xrb[:, (m % 2) * S:(m % 2) * S + S],
                                         xst[:, m * S:(m + 1) * S]).then_inc(sxr[m % 2], 16)

                    @blk.tensor
                    def _(t):
                        t.wait_ge(s_wqkv, 16 * 4)

                        def emit_stats_sx(k):
                            t.wait_ge(s_v, v0 + 2 * k + 2)
                            ins = None
                            for nn in range(4):
                                ins = t.matmul(ps_stats[0:1, nn * 512:(nn + 1) * 512],
                                               onescol_sb,
                                               x2buf[:, k * S + nn * 512: k * S + (nn + 1) * 512],
                                               start=(k == 0), stop=(k == KD - 1))
                            ins.then_inc(s_pe, 1)
                            n["pe"] += 1

                        def emit_stats_sq(k):
                            t.wait_ge(s_sc, sq0 + k + 1)
                            ins = None
                            for nn in range(4):
                                ins = t.matmul(ps_stats[32:33, nn * 512:(nn + 1) * 512],
                                               onescol_bf,
                                               sqbf2[:, (k % 2) * S + nn * 512: (k % 2) * S + (nn + 1) * 512],
                                               start=(k == 0), stop=(k == KD - 1))
                            ins.then_inc(s_pe, 1)
                            n["pe"] += 1
                            sqstat_pe[k] = n["pe"]

                        for m in range(KD):
                            for hf in range(2):
                                g2 = 2 * m + hf
                                if g2 >= 2:
                                    t.wait_ge(s_v, v0 + g2 - 1)
                                ins = None
                                for j in range(NPAIR):
                                    for nn in range(2):
                                        ins = t.matmul(
                                            ps_wo[:, (g2 % 2) * 1024 + nn * 512:(g2 % 2) * 1024 + (nn + 1) * 512],
                                            wo_sb[:, m * 1024 + j * 128: m * 1024 + (j + 1) * 128],
                                            attn_n[:, j * S + hf * 1024 + nn * 512: j * S + hf * 1024 + (nn + 1) * 512],
                                            start=(j == 0), stop=(j == NPAIR - 1))
                                ins.then_inc(s_pe, 1)
                                n["pe"] += 1
                            if m >= 1:
                                emit_stats_sx(m - 1)
                            if m >= 3:
                                emit_stats_sq(m - 3)
                        emit_stats_sx(KD - 1)
                        for k in range(KD - 3, KD):
                            emit_stats_sq(k)

                    @blk.vector
                    def _(v):
                        for m in range(KD):
                            for hf in range(2):
                                g2 = 2 * m + hf
                                v.wait_ge(s_pe, pe0 + g2 + 1 + max(m - 1, 0) + max(m - 3, 0))
                                v.wait_ge(sxr[m % 2], 16 * (m // 2 + 1))
                                vop(v, "scalar_tensor_tensor",
                                    x2buf[:, m * S + hf * 1024: m * S + hf * 1024 + 1024],
                                    ps_wo[:, (g2 % 2) * 1024:(g2 % 2) * 1024 + 1024],
                                    bo_sb[:, m:m + 1],
                                    xrb[:, (m % 2) * S + hf * 1024:(m % 2) * S + (hf + 1) * 1024],
                                    ALU.add, ALU.add)

                    @blk.scalar
                    def _(sc):
                        for k in range(KD):
                            sc.wait_ge(s_v, v0 + 2 * k + 2)
                            if k >= 2:
                                sc.wait_ge(s_pe, sqstat_pe[k - 2])
                            scop(sc, "activation", sqbf2[:, (k % 2) * S:(k % 2) * S + S],
                                 x2buf[:, k * S:(k + 1) * S], AF.Square)

                    @blk.sync
                    def _(sy):
                        for m in range(KD):
                            sy.wait_ge(s_v, v0 + 2 * m + 2)
                            sy.dma_start(x2hbm[:, m * S:(m + 1) * S],
                                         x2buf[:, m * S:(m + 1) * S]).then_inc(s_x2st, 16)

            with nc.Block() as blk:
                    @blk.vector
                    def _(v):
                        v.wait_ge(s_pe, n["pe"])
                        rowm = rowbuf2[0:1, 0:S]
                        rowr = rowbuf2[0:1, S:2 * S]
                        rowu = xsq2[0:1, S:2 * S]
                        vop(v, "tensor_scalar_mul", rowm, ps_stats[0:1, :], 1.0 / D)
                        vop(v, "tensor_mul", rowu, rowm, rowm)
                        vop(v, "scalar_tensor_tensor", rowr, ps_stats[32:33, :], 1.0 / D,
                            rowu, ALU.mult, ALU.subtract)
                        vop(v, "tensor_scalar_add", rowu, rowr, LN_EPS)

                    @blk.scalar
                    def _(sc):
                        rowt = xsq2[0:1, 0:S]
                        rowu = xsq2[0:1, S:2 * S]
                        sc.wait_ge(s_v, n["v"])
                        scop(sc, "activation", rowt, rowu, AF.Ln)
                        scop(sc, "activation", rowt, rowt, AF.Exp, scale=-0.5)

                    @blk.vector
                    def _(v):
                        rowt = xsq2[0:1, 0:S]
                        rowu = xsq2[0:1, S:2 * S]
                        rowr = rowbuf2[0:1, S:2 * S]
                        v.wait_ge(s_sc, n["sc"])
                        vop(v, "tensor_mul", rowr, rowt, rowt)
                        vop(v, "tensor_mul", rowr, rowr, rowu)
                        vop(v, "tensor_scalar", rowr, rowr, -0.5, 1.5, ALU.mult, ALU.add)
                        vop(v, "tensor_mul", rowr, rowt, rowr)

                    vrows = n["v"]

        wb = [wb0, wb1, wb2]
        sched = []
        for hf in range(2):
            sched += [("w1", hf, m) for m in range(KH)]
            sched += [("w2", hf, m) for m in range(KH)]
            sched += [("h1", hf, m) for m in range(KH)]
            sched += [("h2", hf, m) for m in range(KH)]
            sched += [("w3", hf, m) for m in range(KD)]
        NS = len(sched)
        wsrc = {"w1": (w1_d, KD), "w2": (w2_d, KD), "h1": (h1_d, KH),
                "h2": (h2_d, KH), "w3": (w3_d, KH)}
        eng = {"w1": "s", "w2": "v", "h1": "s", "h2": "s", "w3": "v"}
        evs_b = [0] * (NS + 1)
        evv_b = [0] * (NS + 1)
        for i, (lay, hf, m) in enumerate(sched):
            evs_b[i + 1] = evs_b[i] + (1 if eng[lay] == "s" else 0)
            evv_b[i + 1] = evv_b[i] + (1 if eng[lay] == "v" else 0)

        mm0 = n["mm"]
        wc0 = n["wc"]
        evs0 = n["evs"]
        evv0 = n["evv"]
        w3_evv_val = [evv0 + evv_b[i] + 1
                      for i, (lay, hf, m) in enumerate(sched) if lay == "w3"]

        with (
            nc.psum_tensor("ps_f", [128, 2 * 1024], F32) as ps_f,
            nc.psum_tensor("ps_m2", [128, 1024], F32) as ps_m2,
            nc.psum_tensor("ps_r2", [128, 1024], F32) as ps_r2,
        ):
            with nc.Block() as blk:
                pe_rep = [0, 0]
                vnh = [0, 0]
                vnk = [[0] * KD, [0] * KD]
                rowm = rowbuf2[0:1, 0:S]
                rowr2 = rowbuf2[0:1, S:2 * S]

                def emit_rep2(t, hf):
                    ins = None
                    for nn in range(2):
                        t.matmul(ps_m2[:, nn * 512:(nn + 1) * 512], ones_sb,
                                 rowm[0:1, hf * 1024 + nn * 512: hf * 1024 + (nn + 1) * 512],
                                 start=True, stop=True)
                    for nn in range(2):
                        ins = t.matmul(ps_r2[:, nn * 512:(nn + 1) * 512], ones_sb,
                                       rowr2[0:1, hf * 1024 + nn * 512: hf * 1024 + (nn + 1) * 512],
                                       start=True, stop=True)
                    ins.then_inc(s_pe, 1)
                    n["pe"] += 1
                    pe_rep[hf] = n["pe"]

                def emit_norm2(v, hf):
                    v.wait_ge(s_pe, pe_rep[hf])
                    for k in range(KD):
                        vop(v, "tensor_sub",
                            act_bf[:, k * S + hf * 1024: k * S + hf * 1024 + 1024],
                            x2buf[:, k * S + hf * 1024: k * S + hf * 1024 + 1024],
                            ps_m2[:])
                        vnk[hf][k] = vop(v, "tensor_mul",
                            act_bf[:, k * S + hf * 1024: k * S + hf * 1024 + 1024],
                            act_bf[:, k * S + hf * 1024: k * S + hf * 1024 + 1024],
                            ps_r2[:])
                    vnh[hf] = n["v"]

                @blk.tensor
                def _(t):
                    t.wait_ge(s_v, vrows)
                    emit_rep2(t, 0)

                @blk.vector
                def _(v):
                    emit_norm2(v, 0)

                pe_rep[1] = pe_rep[0] + 1

                @blk.vector
                def _(v):
                    emit_norm2(v, 1)

                @blk.gpsimd
                def _(g):
                    for i, (lay, hf, m) in enumerate(sched):
                        if i < 3:
                            continue
                        src, kc = wsrc[lay]
                        g.wait_ge(s_mm, mm0 + i - 2)
                        g.dma_start(wb[i % 3][:, 0: kc * 128], src[m]).then_inc(swl[i % 3], 16)

                @blk.tensor
                def _(t):
                    for i, (lay, hf, m) in enumerate(sched):
                        src, kc = wsrc[lay]
                        if i == 2:
                            t.wait_ge(s_v, vnh[0])
                            emit_rep2(t, 1)
                        t.wait_ge(swl[i % 3], 16 * (i // 3 + 1))
                        if m == 0:
                            if evs_b[i] > 0:
                                t.wait_ge(s_evs, evs0 + evs_b[i])
                            if evv_b[i] > 0:
                                t.wait_ge(s_evv, evv0 + evv_b[i])
                        elif i >= 2:
                            lay2 = sched[i - 2][0]
                            if eng[lay2] == "s":
                                t.wait_ge(s_evs, evs0 + evs_b[i - 2] + 1)
                            else:
                                t.wait_ge(s_evv, evv0 + evv_b[i - 2] + 1)
                        if lay in ("w1", "w2"):
                            rhs, rb, rstride = act_bf, hf * THALF, S
                        elif lay == "h1":
                            rhs, rb, rstride = hbufB, 0, THALF
                        elif lay == "h2":
                            rhs, rb, rstride = hbufA, 0, THALF
                        else:
                            rhs, rb, rstride = hbufB, 0, THALF
                        ins = None
                        for k in range(kc):
                            if lay == "w1" and m == 0:
                                t.wait_ge(s_v, vnk[hf][k])
                            for nn in range(2):
                                ins = t.matmul(
                                    ps_f[:, (i % 2) * 1024 + nn * 512:(i % 2) * 1024 + (nn + 1) * 512],
                                    wb[i % 3][:, k * 128:(k + 1) * 128],
                                    rhs[:, k * rstride + rb + nn * 512: k * rstride + rb + (nn + 1) * 512],
                                    start=(k == 0), stop=(k == kc - 1))
                        ins.then_inc(s_mm, 1)

                n["wc"] += NS
                n["mm"] += NS

                @blk.scalar
                def _(sc):
                    sc.wait_ge(s_x2st, 16 * KD)
                    for i, (lay, hf, m) in enumerate(sched):
                        if eng[lay] != "s":
                            continue
                        sc.wait_ge(s_mm, mm0 + i + 1)
                        if lay == "w1" and hf == 0 and m == 8:
                            sc.wait_ge(s_pe, pe_rep[1])
                        if lay == "w1" and hf == 0 and m == 16:
                            sc.wait_ge(s_v, vnh[1])
                        dst = hbufA if lay in ("w1", "h1") else hbufB
                        func = AF.Silu if lay == "w1" else AF.Gelu
                        ssb = {"w1": s1_sb, "h1": sh1_sb, "h2": sh2_sb}[lay]
                        sc.activation(dst[:, m * THALF:(m + 1) * THALF],
                                      ps_f[:, (i % 2) * 1024:(i % 2) * 1024 + 1024],
                                      func, scale=ssb[:, m:m + 1]).then_inc(s_evs, 1)
                        n["evs"] += 1

                @blk.vector
                def _(v):
                    v.wait_ge(s_x2st, 16 * KD)
                    nxr = 0
                    nst = 0
                    for i, (lay, hf, m) in enumerate(sched):
                        if eng[lay] != "v":
                            continue
                        v.wait_ge(s_mm, mm0 + i + 1)
                        v.wait_ge(s_v, n["v"])
                        if lay == "w2":
                            v.scalar_tensor_tensor(
                                hbufB[:, m * THALF:(m + 1) * THALF],
                                ps_f[:, (i % 2) * 1024:(i % 2) * 1024 + 1024],
                                s2_sb[:, m:m + 1],
                                hbufA[:, m * THALF:(m + 1) * THALF],
                                ALU.mult, ALU.mult).then_inc(s_evv, 1)
                            n["evv"] += 1
                        else:
                            v.wait_ge(sxr[nxr % 2], 16 * (KD // 2) + 16 * (nxr // 2 + 1))
                            if nst >= 2:
                                v.wait_ge(sout[nst % 2], 16 * (nst // 2))
                            v.scalar_tensor_tensor(
                                ostg[:, (nst % 2) * 1024:(nst % 2) * 1024 + 1024],
                                ps_f[:, (i % 2) * 1024:(i % 2) * 1024 + 1024],
                                s3_sb[:, m:m + 1],
                                x2rb[:, (nxr % 2) * 1024:(nxr % 2) * 1024 + 1024],
                                ALU.mult, ALU.add).then_inc(s_evv, 1)
                            n["evv"] += 1
                            nxr += 1
                            nst += 1

                @blk.sync
                def _(sy):
                    nxr = 0
                    nst = 0
                    for i, (lay, hf, m) in enumerate(sched):
                        if lay != "w3":
                            continue
                        if nxr == 0:
                            sy.wait_ge(s_x2st, 16 * KD)
                        if nxr >= 2:
                            sy.wait_ge(s_evv, w3_evv_val[nxr - 2])
                        sy.dma_start(x2rb[:, (nxr % 2) * 1024:(nxr % 2) * 1024 + 1024],
                                     x2hbm[:, m * S + hf * 1024: m * S + hf * 1024 + 1024]
                                     ).then_inc(sxr[nxr % 2], 16)
                        nxr += 1
                        sy.wait_ge(s_evv, w3_evv_val[nst])
                        sy.dma_start(out_fm[:, m * S + hf * 1024: m * S + hf * 1024 + 1024],
                                     ostg[:, (nst % 2) * 1024:(nst % 2) * 1024 + 1024]
                                     ).then_inc(sout[nst % 2], 16)
                        nst += 1
                    sy.wait_ge(sout[0], 16 * 8)
                    sy.wait_ge(sout[1], 16 * 8)

    return nc


def kernel(**inputs):
    ins = inputs
    x = np.asarray(ins["L0_attn_x"], np.float32)

    def stage_w(wmat):
        WT = np.ascontiguousarray(np.asarray(wmat, np.float32).T)
        kin, kout = WT.shape
        KC, MT = kin // 128, kout // 128
        return np.ascontiguousarray(
            WT.reshape(KC, 128, MT, 128).transpose(2, 1, 0, 3).reshape(MT, 128, KC * 128))

    wq_m = np.asarray(ins["L0_attn_wq"], np.float32).transpose(1, 0, 2).reshape(D, D)
    wk_m = np.asarray(ins["L0_attn_wk"], np.float32).transpose(1, 0, 2).reshape(D, D)
    wv_m = np.asarray(ins["L0_attn_wv"], np.float32).transpose(1, 0, 2).reshape(D, D)
    wq_st = np.ascontiguousarray(
        wq_m.reshape(KD, 128, NPAIR, 128).transpose(1, 2, 0, 3).reshape(128, 8192))
    wk_st = np.ascontiguousarray(
        wk_m.reshape(KD, 128, NPAIR, 128).transpose(1, 2, 0, 3).reshape(128, 8192))
    wv_st = np.ascontiguousarray(
        wv_m.reshape(KD, 128, D).transpose(1, 0, 2).reshape(128, 8192))
    woT = np.ascontiguousarray(np.asarray(ins["L0_attn_wo"], np.float32).T)
    wo_st = np.ascontiguousarray(
        woT.reshape(KD, 128, KD, 128).transpose(1, 2, 0, 3).reshape(128, 8192))

    bq_st = np.ascontiguousarray(
        np.asarray(ins["L0_attn_bq"], np.float32).reshape(NPAIR, 2, 64)
        .transpose(1, 2, 0).reshape(128, NPAIR))
    bk_st = np.ascontiguousarray(
        np.asarray(ins["L0_attn_bk"], np.float32).reshape(NPAIR, 2, 64)
        .transpose(1, 2, 0).reshape(128, NPAIR))
    bv_st = np.ascontiguousarray(np.asarray(ins["L0_attn_bv"], np.float32).reshape(1, D))
    bo_st = np.ascontiguousarray(np.asarray(ins["L0_attn_bo"], np.float32).reshape(KD, 128).T)

    w1_st = stage_w(ins["L0_ffn_w1_w"])
    w2_st = stage_w(ins["L0_ffn_w2_w"])
    h1_st = stage_w(ins["L0_ffn_h1_w"])
    h2_st = stage_w(ins["L0_ffn_h2_w"])
    w3_st = stage_w(ins["L0_ffn_w3_w"])
    s1_st = np.ascontiguousarray(np.asarray(ins["L0_ffn_w1_s"], np.float32).reshape(KH, 128).T)
    s2_st = np.ascontiguousarray(np.asarray(ins["L0_ffn_w2_s"], np.float32).reshape(KH, 128).T)
    sh1_st = np.ascontiguousarray(np.asarray(ins["L0_ffn_h1_s"], np.float32).reshape(KH, 128).T)
    sh2_st = np.ascontiguousarray(np.asarray(ins["L0_ffn_h2_s"], np.float32).reshape(KH, 128).T)
    s3_st = np.ascontiguousarray(np.asarray(ins["L0_ffn_w3_s"], np.float32).reshape(KD, 128).T)

    nc = build_nc()
    in_maps = []
    for c in range(B):
        xc = np.ascontiguousarray(
            x[c].T.reshape(KD, 128, S).transpose(1, 0, 2).reshape(128, KD * S))
        in_maps.append({
            "xst": xc, "wq": wq_st, "wk": wk_st, "wv": wv_st, "wo": wo_st,
            "bq": bq_st, "bk": bk_st, "bvr": bv_st, "bo": bo_st,
            "w1": w1_st, "w2": w2_st, "h1": h1_st, "h2": h2_st, "w3": w3_st,
            "s1": s1_st, "s2": s2_st, "sh1": sh1_st, "sh2": sh2_st, "s3": s3_st,
        })
    res = run_bass_kernel_spmd(nc, in_maps, core_ids=list(range(B)))
    outs = []
    for c in range(B):
        o = np.asarray(res.results[c]["out_fm"], np.float32)
        outs.append(o.reshape(128, KD, S).transpose(1, 0, 2).reshape(D, S).T)
    return np.ascontiguousarray(np.stack(outs, 0))
